# version 06
# baseline (speedup 1.0000x reference)
from contextlib import ExitStack

import numpy as np
import ml_dtypes

import concourse.bass as bass
import concourse.mybir as mybir
from concourse.bass_utils import run_bass_kernel_spmd

F32 = mybir.dt.float32
BF16 = mybir.dt.bfloat16
AF = mybir.ActivationFunctionType
ALU = mybir.AluOpType
AX = mybir.AxisListType

H = 4096
DH = 1024
NT = 1024
S = 4096
NCH = NT // 128
DCH = H // 128
EPS = 1e-5
GROUPS = [[0, 1, 2, 3], [4, 5, 6, 7]]

DEBUG_TAPS = ()
UPTO = "D"

ENGS = ("tensor", "vector", "scalar", "gpsimd", "sync")


class Prog:

    def __init__(self, nc, stack):
        self.nc = nc
        self.stack = stack
        self.streams = {e: [] for e in ENGS}
        self.cnt = {}
        self.hdl = {}
        self.nblk = 0

    def sem(self, name):
        self.hdl[name] = self.stack.enter_context(self.nc.semaphore(name))
        self.cnt[name] = 0

    def op(self, eng, builder, inc=None, k=1, drain=False):
        if inc is not None and drain:
            assert eng == "tensor"
            self.cnt[inc] += k
            h = self.hdl[inc]

            def emit(e, b=builder, h=h, k=k):
                b(e)
                e.drain().then_inc(h, k)
            self.streams[eng].append(emit)
            return self.cnt[inc]
        if inc is None:
            self.streams[eng].append(lambda e, b=builder: b(e))
            return None
        self.cnt[inc] += k
        h = self.hdl[inc]
        self.streams[eng].append(lambda e, b=builder, h=h, k=k: b(e).then_inc(h, k))
        return self.cnt[inc]

    def delay(self, eng, scratch):
        if eng == "vector":
            self.streams[eng].append(
                lambda e, s=scratch: e.tensor_copy(s[0:1, 0:1], s[0:1, 1:2]))
        else:
            self.streams[eng].append(
                lambda e, s=scratch: e.copy(s[0:1, 0:1], s[0:1, 1:2]))

    def wait(self, eng, sname, val):
        if val is None or val <= 0:
            return
        h = self.hdl[sname]
        self.streams[eng].append(lambda e, h=h, v=val: e.wait_ge(h, v))

    def barrier(self):
        self.nblk += 1
        streams, self.streams = self.streams, {e: [] for e in ENGS}
        with self.nc.Block(f"ph{self.nblk}") as blk:
            for eng in ENGS:
                fns = streams[eng]
                if not fns:
                    continue
                getattr(blk, eng)(lambda e, fns=fns: [f(e) for f in fns])


def build_nc(taps=(), upto="D"):
    nc = bass.Bass("TRN2", target_bir_lowering=False)

    d_xT = nc.dram_tensor("xT", [H, NT], F32, kind="ExternalInput").ap()
    d_Wqkv = nc.dram_tensor("Wqkv", [H, 3 * DH], F32, kind="ExternalInput").ap()
    d_bqkv = nc.dram_tensor("bqkv_pc", [128, 24], F32, kind="ExternalInput").ap()
    d_bv = nc.dram_tensor("bv_row", [1, DH], F32, kind="ExternalInput").ap()
    d_Wo = nc.dram_tensor("Wo", [DH, H], F32, kind="ExternalInput").ap()
    d_bo = nc.dram_tensor("bo_pc", [128, 32], F32, kind="ExternalInput").ap()
    d_WbT = nc.dram_tensor("WbT", [H, H], F32, kind="ExternalInput").ap()
    d_bb = nc.dram_tensor("bb_row", [1, H], F32, kind="ExternalInput").ap()
    d_WgT = nc.dram_tensor("WgT", [H, 8], F32, kind="ExternalInput").ap()
    d_A2 = nc.dram_tensor("A2", [H, 128], F32, kind="ExternalInput").ap()
    d_BT = nc.dram_tensor("BT", [128, H], F32, kind="ExternalInput").ap()
    d_EXP8 = nc.dram_tensor("EXP8", [8, 128], F32, kind="ExternalInput").ap()
    d_idb = nc.dram_tensor("identb", [128, 128], BF16, kind="ExternalInput").ap()
    d_idf = nc.dram_tensor("identf", [128, 128], F32, kind="ExternalInput").ap()
    d_onc = nc.dram_tensor("onescol", [128, 1], BF16, kind="ExternalInput").ap()
    d_onr = nc.dram_tensor("onesrow", [1, 128], BF16, kind="ExternalInput").ap()
    d_onr512 = nc.dram_tensor("onesrow512", [1, 512], BF16, kind="ExternalInput").ap()
    d_bor = nc.dram_tensor("bo_row", [1, H], F32, kind="ExternalInput").ap()
    d_onrf = nc.dram_tensor("onesrowf", [1, 128], F32, kind="ExternalInput").ap()
    d_out = nc.dram_tensor("out", [NT, H], F32, kind="ExternalOutput").ap()

    cc_in = nc.dram_tensor("cc_in", [2 * DH, NT], BF16, kind="Internal")
    cc_out = nc.dram_tensor("cc_out", [8 * 2 * DH, NT], BF16, kind="Internal")
    d_kvoff = nc.dram_tensor("kvoff", [1, 4], mybir.dt.int32,
                             kind="ExternalInput").ap()

    tap_shapes = {
        "Kown": [DH, NT], "Vown": [NT, DH], "Qb": [DH, NT],
        "attn": [NT, DH], "y1T": [H, NT], "ln1T": [H, NT],
        "route": [NT, 8], "hwT": [128, NT], "y2": [NT, H],
    }
    tap_ap = {t: nc.dram_tensor("tap_" + t, tap_shapes[t], F32,
                                kind="ExternalOutput").ap() for t in taps}

    with ExitStack() as stack:
        ent = stack.enter_context
        P = Prog(nc, stack)
        for s in ("sX", "sW", "sSY", "sPE", "sEV", "sDV", "sCC", "sST", "sPL",
                  "sW0", "sW1", "sWq0", "sWq1", "sXF0", "sXF1", "sKV0", "sKV1",
                  "sSO0", "sSO1"):
            P.sem(s)

        SL1 = ent(nc.sbuf_tensor("SL1", [128, 32768], BF16))
        SL2 = ent(nc.sbuf_tensor("SL2", [128, 32768], BF16))
        bq_pc = ent(nc.sbuf_tensor("bq_pc", [128, 24], F32))
        bo_pc = ent(nc.sbuf_tensor("sb_bo_pc", [128, 32], F32))
        identb = ent(nc.sbuf_tensor("sb_identb", [128, 128], BF16))
        identf = ent(nc.sbuf_tensor("sb_identf", [128, 128], F32))
        onesc = ent(nc.sbuf_tensor("onesc", [128, 1], BF16))
        onesr = ent(nc.sbuf_tensor("onesr", [1, 128], BF16))
        onesr512 = ent(nc.sbuf_tensor("onesr512", [1, 512], BF16))
        onesrf = ent(nc.sbuf_tensor("onesrf", [1, 128], F32))
        WgTb = ent(nc.sbuf_tensor("WgTb", [128, 32 * 8], BF16))
        EXP8b = ent(nc.sbuf_tensor("EXP8b", [8, 128], F32))
        dly = ent(nc.sbuf_tensor("dly", [1, 8], F32))
        kvoffb = ent(nc.sbuf_tensor("kvoffb", [1, 4], mybir.dt.int32))
        stackAB = ExitStack()
        Qb = stackAB.enter_context(nc.sbuf_tensor("Qb", [128, 8192], BF16))
        Kown = stackAB.enter_context(nc.sbuf_tensor("Kown", [128, 8192], BF16))
        rk = stackAB.enter_context(nc.gpsimd.register("rk"))
        rv = stackAB.enter_context(nc.gpsimd.register("rv"))

        xTb = SL1[:, :]
        wkq = [SL2[:, i * 4096:(i + 1) * 4096] for i in range(4)]
        wv = [SL2[:, 16384 + i * 8192: 16384 + (i + 1) * 8192] for i in range(2)]

        sA = ExitStack()
        arena = sA.enter_context(nc.psum_tensor("arena", [128, 8, 512], F32))
        Vown = sA.enter_context(nc.sbuf_tensor("Vown", [128, 8192], BF16))
        bvb = sA.enter_context(nc.sbuf_tensor("bvb", [1, DH], BF16))
        if True:
            P.op("gpsimd", lambda e: e.dma_start(
                out=xTb.rearrange("p (c n) -> p c n", c=32),
                in_=d_xT.rearrange("(c p) n -> p c n", p=128)), inc="sX", k=16)
            P.op("gpsimd", lambda e: e.dma_start(out=bvb[:, :], in_=d_bv),
                 inc="sX", k=16)
            P.op("gpsimd", lambda e: e.dma_start(
                out=WgTb.rearrange("p (c f) -> p c f", c=32),
                in_=d_WgT.rearrange("(c p) f -> p c f", p=128)), inc="sX", k=16)
            x_done = bv_done = P.cnt["sX"]

            for dst, src in ((bq_pc, d_bqkv), (bo_pc, d_bo), (identb, d_idb),
                             (identf, d_idf), (onesc, d_onc), (onesr, d_onr),
                             (onesrf, d_onrf), (EXP8b, d_EXP8),
                             (onesr512, d_onr512), (kvoffb, d_kvoff)):
                P.op("sync", lambda e, a=dst, b=src: e.dma_start(out=a[:, :], in_=b),
                     inc="sSY", k=16)
            sy_const = P.cnt["sSY"]

            pe_stripe = {}
            stripe_w = {}
            bank_ev = {}

            def load_stripe(seq, ci):
                slot = seq % 4
                prev = pe_stripe.get(seq - 4)
                if prev is not None:
                    P.wait("gpsimd", "sPE", prev)
                sem = f"sW{slot % 2}" if slot < 2 else f"sWq{slot - 2}"
                P.op("gpsimd", lambda e, ci=ci, s=slot: e.dma_start(
                    out=wkq[s].rearrange("p (c f) -> p c f", c=32),
                    in_=d_Wqkv[:, ci * 128:(ci + 1) * 128]
                    .rearrange("(c p) f -> p c f", p=128)), inc=sem, k=16)
                stripe_w[seq] = (sem, P.cnt[sem])

            def proj_ft(kind, idx, ci, dst, scale):
                seq = idx if kind == "K" else 8 + idx
                slot = seq % 4
                if seq not in stripe_w:
                    load_stripe(seq, ci)
                P.wait("tensor", stripe_w[seq][0], stripe_w[seq][1])
                if kind == "K" and idx == 0:
                    P.wait("tensor", "sX", x_done)
                if kind == "Q" and idx == 0:
                    P.wait("tensor", "sEV", kv_evac_total[0])
                for ntile in range(2):
                    bank = (2 * idx + ntile) % 4 + (4 if kind == "Q" else 0)
                    P.wait("tensor", "sEV", bank_ev.get(bank))
                    pe_here = None
                    for d in range(DCH):
                        pe_here = P.op(
                            "tensor", lambda e, s=slot, d=d, nt=ntile, b=bank: e.matmul(
                                arena[:, b, :],
                                wkq[s][:, d * 128:(d + 1) * 128],
                                xTb[:, d * 1024 + nt * 512: d * 1024 + nt * 512 + 512],
                                start=(d == 0), stop=(d == DCH - 1)),
                            inc="sPE" if d == DCH - 1 else None,
                            drain=(d == DCH - 1))
                    P.wait("vector", "sPE", pe_here)
                    if kind == "Q" and idx == 7 and ntile == 1:
                        P.delay("vector", dly)
                    P.wait("vector", "sSY", sy_const)
                    bias_ap = bq_pc[:, ci:ci + 1]
                    bank_ev[bank] = P.op(
                        "vector", lambda e, dst=dst, i=idx, nt=ntile, b=bank,
                        ba=bias_ap, sc=scale: e.tensor_scalar(
                            dst[:, i * 1024 + nt * 512: i * 1024 + nt * 512 + 512],
                            arena[:, b, :], ba, sc, ALU.add, ALU.mult), inc="sEV")
                pe_stripe[seq] = P.cnt["sPE"]

            kv_evac_total = [0]
            for i in range(8):
                proj_ft("K", i, 8 + i, Kown, 1.0)
            k_evac_total = P.cnt["sEV"]

            for dt_ in range(2):
                for hhalf in range(2):
                    vidx = 2 * dt_ + hhalf
                    slot = vidx % 2
                    prev = pe_stripe.get(("V", vidx - 2))
                    if prev is not None:
                        P.wait("gpsimd", "sPE", prev)
                    P.op("gpsimd", lambda e, dt_=dt_, h=hhalf, s=slot: e.dma_start(
                        out=wv[s].rearrange("p (c f) -> p c f", c=16),
                        in_=d_Wqkv[h * 2048:(h + 1) * 2048,
                                   2048 + dt_ * 512: 2048 + (dt_ + 1) * 512]
                        .rearrange("(c p) f -> p c f", p=128)), inc=f"sW{slot}", k=16)
                    P.wait("tensor", f"sW{slot}", P.cnt[f"sW{slot}"])
                    if vidx == 0:
                        P.wait("tensor", "sEV", k_evac_total)
                    if dt_ == 1 and hhalf == 0:
                        P.wait("tensor", "sEV", P.cnt["sEV"])
                    pe_here = None
                    for d in range(16):
                        dg = hhalf * 16 + d
                        for ncg in range(8):
                            last = (d == 15 and ncg == 7)
                            pe_here = P.op(
                                "tensor",
                                lambda e, s=slot, d=d, dg=dg, ncg=ncg: e.matmul(
                                    arena[:, ncg, :],
                                    xTb[:, dg * 1024 + ncg * 128:
                                        dg * 1024 + ncg * 128 + 128],
                                    wv[s][:, d * 512:(d + 1) * 512],
                                    start=(dg == 0), stop=False),
                                inc="sPE" if last else None)
                    pe_stripe[("V", vidx)] = pe_here
                P.wait("tensor", "sX", bv_done)
                for ncg in range(8):
                    pe_here = P.op("tensor", lambda e, ncg=ncg, dt_=dt_: e.matmul(
                        arena[:, ncg, :], onesr[0:1, :],
                        bvb[0:1, dt_ * 512:(dt_ + 1) * 512],
                        start=False, stop=True), inc="sPE", drain=True)
                    P.wait("scalar", "sPE", pe_here)
                    P.op("scalar", lambda e, ncg=ncg, dt_=dt_: e.activation(
                        Vown[:, ncg * 1024 + dt_ * 512: ncg * 1024 + dt_ * 512 + 512],
                        arena[:, ncg, :], AF.Copy), inc="sEV")
            kv_evac_total[0] = P.cnt["sEV"]

            for qi in range(4):
                load_stripe(8 + qi, qi)

            P.wait("gpsimd", "sEV", kv_evac_total[0])
            P.op("gpsimd", lambda e: e.dma_start(
                out=cc_in.ap()[0:DH, :].rearrange("(c p) n -> p c n", p=128),
                in_=Kown.rearrange("p (c n) -> p c n", c=8)), inc="sW", k=16)
            P.op("gpsimd", lambda e: e.dma_start(
                out=cc_in.ap()[DH:2 * DH, :].rearrange("(c p) n -> p c n", p=128),
                in_=Vown.rearrange("p (c n) -> p c n", c=8)), inc="sW", k=16)

            P.wait("gpsimd", "sW", P.cnt["sW"])
            P.op("gpsimd", lambda e: e.collective_compute(
                "AllGather", ALU.bypass, replica_groups=[list(range(8))],
                ins=[cc_in.ap().opt()], outs=[cc_out.ap().opt()]), inc="sCC")

            for i in range(8):
                proj_ft("Q", i, i, Qb, 1.0 / 32.0)

            if "Kown" in tap_ap:
                P.wait("gpsimd", "sEV", kv_evac_total[0])
                P.op("gpsimd", lambda e: e.dma_start(
                    out=tap_ap["Kown"].rearrange("(c p) n -> p c n", p=128),
                    in_=Kown.rearrange("p (c n) -> p c n", c=8)), inc="sST", k=16)
                P.op("gpsimd", lambda e: e.dma_start(
                    out=tap_ap["Vown"].rearrange("(c p) n -> p c n", p=128),
                    in_=Vown.rearrange("p (c n) -> p c n", c=8)), inc="sST", k=16)
            if "Qb" in tap_ap:
                P.wait("gpsimd", "sEV", P.cnt["sEV"])
                P.op("gpsimd", lambda e: e.dma_start(
                    out=tap_ap["Qb"].rearrange("(c p) n -> p c n", p=128),
                    in_=Qb.rearrange("p (c n) -> p c n", c=8)), inc="sST", k=16)

            if upto == "A":
                P.wait("gpsimd", "sST", P.cnt["sST"])
            P.barrier()

        sA.close()
        if upto == "A":
            stackAB.close()
            return nc

        attnTf = SL1[:, 0:8192]
        attn_bf = SL1[:, 8192:16384]
        Kblk = [SL2[:, i * 8192:(i + 1) * 8192] for i in range(2)]
        Vblk = [SL2[:, 16384 + i * 8192: 16384 + (i + 1) * 8192] for i in range(2)]

        sB = ExitStack()
        acc = sB.enter_context(nc.sbuf_tensor("acc", [128, 8192], F32))
        Pb = sB.enter_context(nc.sbuf_tensor("Pb", [128, 2, 1024], BF16))
        PT = sB.enter_context(nc.sbuf_tensor("PT", [128, 2, 1024], BF16))
        mrun = sB.enter_context(nc.sbuf_tensor("mrun", [128, 8], F32))
        srun = sB.enter_context(nc.sbuf_tensor("srun", [128, 8], F32))
        mb2 = sB.enter_context(nc.sbuf_tensor("mb2", [128, 2, 2], F32))
        mnew = sB.enter_context(nc.sbuf_tensor("mnew", [128, 2], F32))
        negm = sB.enter_context(nc.sbuf_tensor("negm", [128, 2], F32))
        alpha = sB.enter_context(nc.sbuf_tensor("alpha", [128, 2], F32))
        sblk = sB.enter_context(nc.sbuf_tensor("sblk", [128, 2, 2], F32))
        rsum = sB.enter_context(nc.sbuf_tensor("rsum", [128, 1], F32))
        att_ps = sB.enter_context(nc.psum_tensor("att_ps", [128, 6, 512], F32))
        pt_ps = sB.enter_context(nc.psum_tensor("pt_ps", [128, 2, 1024], BF16))
        if True:

            kv_loaded = {}
            pe_blk = {}

            kv_pattern = [[1024, 128], [131072, 8], [1, 1024]]

            def load_kv(r):
                s = r % 2
                P.op("gpsimd", lambda e, r=r: e.reg_load(rk, kvoffb[0:1, r:r + 1]))
                P.op("gpsimd", lambda e: e.reg_add(rv, rk, DH * NT))
                if r > 0:
                    P.op("gpsimd", lambda e, s=s: e.dma_start(
                        out=Kblk[s].rearrange("p (c n) -> p c n", c=8),
                        in_=bass.AP(cc_out, rk, kv_pattern)), inc=f"sKV{s}", k=16)
                P.op("gpsimd", lambda e, s=s: e.dma_start(
                    out=Vblk[s].rearrange("p (c n) -> p c n", c=8),
                    in_=bass.AP(cc_out, rv, kv_pattern)), inc=f"sKV{s}", k=16)
                kv_loaded[r] = (s, P.cnt[f"sKV{s}"])

            P.wait("gpsimd", "sSY", sy_const)
            P.wait("gpsimd", "sCC", 1)
            load_kv(0)

            dv_bank = {}
            pt_bank = {}
            pb_free = {}
            pe_sc = {}
            exp_ev = {}

            def emit_scores(r, qc):
                kb = kvbuf(r)[0]
                p = qc % 2
                for t in range(2):
                    bank = 2 * p + t
                    fb = dv_bank.get(bank)
                    if fb is not None:
                        P.wait("tensor", fb[0], fb[1])
                    pe_here = None
                    for m in range(8):
                        pe_here = P.op(
                            "tensor",
                            lambda e, kb=kb, m=m, qc=qc, t=t, b=bank: e.matmul(
                                att_ps[:, b, :],
                                Qb[:, m * 1024 + qc * 128:
                                   m * 1024 + qc * 128 + 128],
                                kb[:, m * 1024 + t * 512:
                                   m * 1024 + t * 512 + 512],
                                start=(m == 0), stop=(m == 7)),
                            inc="sPE" if m == 7 else None, drain=(m == 7))
                    pe_sc[(qc, t)] = pe_here

            def emit_soft(r, qc):
                p = qc % 2
                mx_done = None
                for t in range(2):
                    bank = 2 * p + t
                    P.wait("vector", "sPE", pe_sc[(qc, t)])
                    mx_done = P.op("vector", lambda e, p=p, t=t, b=bank: e.reduce_max(
                        mb2[:, p, t:t + 1], att_ps[:, b, :], axis=AX.X), inc="sDV")
                P.wait("vector", "sDV", mx_done)
                if r == 0:
                    P.op("vector", lambda e, qc=qc, p=p: e.reduce_max(
                        mrun[:, qc:qc + 1], mb2[:, p, :], axis=AX.X), inc="sDV")
                    P.wait("vector", "sDV", P.cnt["sDV"])
                    dv_here = P.op("vector", lambda e, qc=qc, p=p: e.tensor_scalar_mul(
                        negm[:, p:p + 1], mrun[:, qc:qc + 1], -1.0), inc="sDV")
                    P.wait("vector", "sDV", dv_here)
                else:
                    P.op("vector", lambda e, p=p: e.reduce_max(
                        mnew[:, p:p + 1], mb2[:, p, :], axis=AX.X), inc="sDV")
                    P.wait("vector", "sDV", P.cnt["sDV"])
                    P.op("vector", lambda e, qc=qc, p=p: e.tensor_max(
                        mnew[:, p:p + 1], mnew[:, p:p + 1], mrun[:, qc:qc + 1]),
                        inc="sDV")
                    P.wait("vector", "sDV", P.cnt["sDV"])
                    dv_here = P.op("vector", lambda e, p=p: e.tensor_scalar_mul(
                        negm[:, p:p + 1], mnew[:, p:p + 1], -1.0), inc="sDV")
                    P.wait("vector", "sDV", dv_here)
                    P.wait("scalar", "sDV", dv_here)
                    al_done = P.op("scalar", lambda e, qc=qc, p=p: e.activation(
                        alpha[:, p:p + 1], mrun[:, qc:qc + 1], AF.Exp,
                        bias=negm[:, p:p + 1]), inc="sEV")
                    P.wait("vector", "sEV", al_done)
                    dv_here = P.op("vector", lambda e, qc=qc, p=p: e.tensor_copy(
                        mrun[:, qc:qc + 1], mnew[:, p:p + 1]), inc="sDV")
                    P.wait("vector", "sDV", dv_here)
                P.wait("scalar", "sDV", P.cnt["sDV"])
                P.wait("scalar", "sPE", pb_free.get(p))
                for t in range(2):
                    bank = 2 * p + t
                    exp_ev[qc] = P.op("scalar", lambda e, p=p, t=t, b=bank:
                                      e.activation(
                                          Pb[:, p, t * 512:(t + 1) * 512],
                                          att_ps[:, b, :], AF.Exp,
                                          bias=negm[:, p:p + 1],
                                          accum_out=sblk[:, p, t:t + 1]),
                                      inc="sEV")
                    dv_bank[bank] = ("sEV", P.cnt["sEV"])
                P.wait("vector", "sEV", exp_ev[qc])
                if r == 0:
                    P.op("vector", lambda e, qc=qc, p=p: e.tensor_add(
                        srun[:, qc:qc + 1], sblk[:, p, 0:1], sblk[:, p, 1:2]),
                        inc="sDV")
                else:
                    dv = P.op("vector", lambda e, qc=qc, p=p: e.scalar_tensor_tensor(
                        srun[:, qc:qc + 1], srun[:, qc:qc + 1], alpha[:, p:p + 1],
                        sblk[:, p, 0:1], ALU.mult, ALU.add), inc="sDV")
                    P.wait("vector", "sDV", dv)
                    P.op("vector", lambda e, qc=qc, p=p: e.tensor_add(
                        srun[:, qc:qc + 1], srun[:, qc:qc + 1], sblk[:, p, 1:2]),
                        inc="sDV")

            def emit_tail(r, qc):
                vb = kvbuf(r)[1]
                p = qc % 2
                if r == 0 and qc == 0:
                    P.wait("tensor", f"sKV{kv_loaded[0][0]}", kv_loaded[0][1])
                P.wait("tensor", "sEV", exp_ev[qc])
                pt_copy = None
                for kc in range(8):
                    pb = kc % 2
                    P.wait("tensor", "sDV", pt_bank.get(pb))
                    pe_here = P.op(
                        "tensor", lambda e, kc=kc, pb=pb, p=p: e.transpose(
                            pt_ps[:, pb, 0:128], Pb[:, p, kc * 128:(kc + 1) * 128],
                            identb[:, :]),
                        inc="sPE", drain=True)
                    P.wait("vector", "sPE", pe_here)
                    pt_copy = P.op(
                        "vector", lambda e, kc=kc, pb=pb, p=p: e.tensor_copy(
                            PT[:, p, kc * 128:(kc + 1) * 128], pt_ps[:, pb, 0:128]),
                        inc="sDV")
                    pt_bank[pb] = pt_copy
                pb_free[p] = P.cnt["sPE"]
                P.wait("tensor", "sDV", pt_copy)
                for t in range(2):
                    bank = 4 + t
                    fb = dv_bank.get(bank)
                    if fb is not None:
                        P.wait("tensor", fb[0], fb[1])
                    pe_here = None
                    for kc in range(8):
                        pe_here = P.op(
                            "tensor",
                            lambda e, vb=vb, kc=kc, t=t, b=bank, p=p: e.matmul(
                                att_ps[:, b, :],
                                PT[:, p, kc * 128:(kc + 1) * 128],
                                vb[:, kc * 1024 + t * 512:
                                   kc * 1024 + t * 512 + 512],
                                start=(kc == 0), stop=(kc == 7)),
                            inc="sPE" if kc == 7 else None, drain=(kc == 7))
                    P.wait("vector", "sPE", pe_here)
                    if r == 0:
                        dv_bank[bank] = ("sDV", P.op(
                            "vector", lambda e, qc=qc, t=t, b=bank: e.tensor_copy(
                                acc[:, qc * 1024 + t * 512:
                                    qc * 1024 + t * 512 + 512],
                                att_ps[:, b, :]), inc="sDV"))
                    else:
                        dv = P.op(
                            "vector", lambda e, qc=qc, t=t, p=p: e.tensor_scalar_mul(
                                acc[:, qc * 1024 + t * 512:
                                    qc * 1024 + t * 512 + 512],
                                acc[:, qc * 1024 + t * 512:
                                    qc * 1024 + t * 512 + 512],
                                alpha[:, p:p + 1]), inc="sDV")
                        P.wait("vector", "sDV", dv)
                        dv_bank[bank] = ("sDV", P.op(
                            "vector", lambda e, qc=qc, t=t, b=bank: e.tensor_add(
                                acc[:, qc * 1024 + t * 512:
                                    qc * 1024 + t * 512 + 512],
                                acc[:, qc * 1024 + t * 512:
                                    qc * 1024 + t * 512 + 512],
                                att_ps[:, b, :]), inc="sDV"))

            def kvbuf(r):
                return (Kown, Vblk[0]) if r == 0 else (Kblk[r % 2], Vblk[r % 2])

            for r in range(4):
                if r + 1 < 4:
                    P.wait("gpsimd", "sPE", pe_blk.get(r - 1))
                    load_kv(r + 1)
                if r == 0:
                    emit_scores(0, 0)
                for qc in range(NCH):
                    emit_soft(r, qc)
                    if qc + 1 < NCH:
                        emit_scores(r, qc + 1)
                    elif r + 1 < 4:
                        P.wait("tensor", f"sKV{kv_loaded[r + 1][0]}",
                               kv_loaded[r + 1][1])
                        emit_scores(r + 1, 0)
                    emit_tail(r, qc)
                pe_blk[r] = P.cnt["sPE"]

            for qc in range(NCH):
                P.wait("vector", "sDV", P.cnt["sDV"])
                dv_here = P.op("vector", lambda e, qc=qc: e.reciprocal(
                    rsum[:, :], srun[:, qc:qc + 1]), inc="sDV")
                P.wait("vector", "sDV", dv_here)
                P.op("vector", lambda e, qc=qc: e.tensor_scalar_mul(
                    attn_bf[:, qc * 1024:(qc + 1) * 1024],
                    acc[:, qc * 1024:(qc + 1) * 1024], rsum[:, 0:1]), inc="sDV")
            attn_done = P.cnt["sDV"]
            P.wait("tensor", "sDV", attn_done)
            for qc in range(NCH):
                for m in range(8):
                    pb = m % 2
                    P.wait("tensor", "sDV", pt_bank.get(pb))
                    pe_here = P.op("tensor", lambda e, qc=qc, m=m, pb=pb: e.transpose(
                        pt_ps[:, pb, 0:128],
                        attn_bf[:, qc * 1024 + m * 128: qc * 1024 + m * 128 + 128],
                        identb[:, :]), inc="sPE", drain=True)
                    P.wait("vector", "sPE", pe_here)
                    if qc == NCH - 1 and m == 7:
                        P.delay("vector", dly)
                    pt_bank[pb] = P.op(
                        "vector", lambda e, qc=qc, m=m, pb=pb: e.tensor_copy(
                            attnTf[:, m * 1024 + qc * 128: m * 1024 + qc * 128 + 128],
                            pt_ps[:, pb, 0:128]), inc="sDV")

            if "attn" in tap_ap:
                P.wait("gpsimd", "sDV", attn_done)
                P.op("gpsimd", lambda e: e.dma_start(
                    out=tap_ap["attn"].rearrange("(c p) n -> p c n", p=128),
                    in_=attn_bf.rearrange("p (c n) -> p c n", c=8)), inc="sST", k=16)

            if upto == "B":
                P.wait("gpsimd", "sST", P.cnt["sST"])
            P.barrier()

        sB.close()
        stackAB.close()
        if upto == "B":
            return nc

        y1T = SL2[:, :]
        sC = ExitStack()
        wo_sb = sC.enter_context(nc.sbuf_tensor("wo_sb", [128, 2, 4096], BF16))
        bo_rowb = sC.enter_context(nc.sbuf_tensor("bo_rowb", [1, H], BF16))
        xf32 = sC.enter_context(nc.sbuf_tensor("xf32", [128, 2, 1024], F32))
        y1sq = sC.enter_context(nc.sbuf_tensor("y1sq", [128, 3, 1024], BF16))
        s1row = sC.enter_context(nc.sbuf_tensor("s1row", [1, 1024], F32))
        s2row = sC.enter_context(nc.sbuf_tensor("s2row", [1, 1024], F32))
        t1row = sC.enter_context(nc.sbuf_tensor("t1row", [1, 1024], F32))
        m1e = sC.enter_context(nc.sbuf_tensor("m1e", [128, 1024], F32))
        r1e = sC.enter_context(nc.sbuf_tensor("r1e", [128, 1024], F32))
        tmpc = sC.enter_context(nc.sbuf_tensor("tmpc", [128, 1024], F32))
        tmpc2 = sC.enter_context(nc.sbuf_tensor("tmpc2", [128, 1024], F32))
        op_ps = sC.enter_context(nc.psum_tensor("op_ps", [128, 4, 512], F32))
        st_ps = sC.enter_context(nc.psum_tensor("st_ps", [1, 4, 512], F32))
        if True:

            wo_loaded = {}
            xf_loaded = {}
            y1_dv = {}
            sq_ev = {}
            stat_pe = {}
            opb_ev = {}

            def load_wo(hb):
                s = hb % 2
                P.op("gpsimd", lambda e, hb=hb, s=s: e.dma_start(
                    out=wo_sb[:, s, :].rearrange("p (c f) -> p c f", c=8),
                    in_=d_Wo[:, hb * 512:(hb + 1) * 512]
                    .rearrange("(c p) f -> p c f", p=128)), inc=f"sW{s}", k=16)
                wo_loaded[hb] = (s, P.cnt[f"sW{s}"])

            def load_xf(h):
                s = h % 2
                if h - 2 >= 0:
                    P.wait("gpsimd", "sDV", y1_dv.get(h - 2))
                P.op("gpsimd", lambda e, h=h, s=s: e.dma_start(
                    out=xf32[:, s, :], in_=d_xT[h * 128:(h + 1) * 128, :]),
                    inc=f"sXF{s}", k=16)
                xf_loaded[h] = (s, P.cnt[f"sXF{s}"])

            P.op("gpsimd", lambda e: e.dma_start(out=bo_rowb[:, :], in_=d_bor),
                 inc="sW", k=16)
            load_wo(0)
            load_wo(1)
            load_xf(0)
            load_xf(1)

            def stats_for(h):
                P.wait("tensor", "sDV", y1_dv[h])
                for t in range(2):
                    P.op("tensor", lambda e, h=h, t=t: e.matmul(
                        st_ps[0:1, t, :], onesc[:, 0:1],
                        y1T[:, h * 1024 + t * 512: h * 1024 + t * 512 + 512],
                        start=(h == 0), stop=(h == DCH - 1)),
                        inc="sPE" if h == DCH - 1 else None,
                        drain=(h == DCH - 1))
                P.wait("tensor", "sEV", sq_ev[h])
                pe_here = None
                for t in range(2):
                    pe_here = P.op("tensor", lambda e, h=h, t=t: e.matmul(
                        st_ps[0:1, 2 + t, :], onesc[:, 0:1],
                        y1sq[:, h % 3, t * 512:(t + 1) * 512],
                        start=(h == 0), stop=(h == DCH - 1)), inc="sPE",
                        drain=(h == DCH - 1))
                stat_pe[h] = pe_here

            for h in range(DCH):
                hb = h // 4
                s = hb % 2
                P.wait("tensor", f"sW{wo_loaded[hb][0]}", wo_loaded[hb][1])
                for t in range(2):
                    bank = (2 * h + t) % 4
                    P.wait("tensor", "sDV", opb_ev.get(bank))
                    for m in range(8):
                        P.op("tensor", lambda e, h=h, s=s, m=m, t=t, b=bank: e.matmul(
                            op_ps[:, b, :],
                            wo_sb[:, s, m * 512 + (h % 4) * 128:
                                  m * 512 + (h % 4) * 128 + 128],
                            attnTf[:, m * 1024 + t * 512: m * 1024 + t * 512 + 512],
                            start=(m == 0), stop=False))
                    pe_here = P.op("tensor", lambda e, h=h, b=bank: e.matmul(
                        op_ps[:, b, :], bo_rowb[0:1, h * 128:(h + 1) * 128],
                        onesr512[0:1, :], start=False, stop=True),
                        inc="sPE", drain=True)
                    P.wait("vector", "sPE", pe_here)
                    P.wait("vector", f"sXF{xf_loaded[h][0]}", xf_loaded[h][1])
                    opb_ev[bank] = P.op(
                        "vector", lambda e, h=h, t=t, b=bank, s=h % 2:
                        e.tensor_add(
                            y1T[:, h * 1024 + t * 512: h * 1024 + t * 512 + 512],
                            op_ps[:, b, :],
                            xf32[:, s, t * 512:(t + 1) * 512]), inc="sDV")
                y1_dv[h] = P.cnt["sDV"]
                if h + 2 < DCH:
                    load_xf(h + 2)
                P.wait("scalar", "sDV", y1_dv[h])
                P.wait("scalar", "sPE", stat_pe.get(h - 3))
                sq_ev[h] = P.op("scalar", lambda e, h=h: e.activation(
                    y1sq[:, h % 3, :], y1T[:, h * 1024:(h + 1) * 1024], AF.Square),
                    inc="sEV")
                if h - 2 >= 0:
                    stats_for(h - 2)
                if h % 4 == 3 and hb + 2 <= 7:
                    P.wait("gpsimd", "sPE", P.cnt["sPE"])
                    load_wo(hb + 2)
            stats_for(DCH - 2)
            stats_for(DCH - 1)
            stats_pe_done = P.cnt["sPE"]

            tap_y1_st = None
            if "y1T" in tap_ap:
                P.wait("gpsimd", "sDV", P.cnt["sDV"])
                tap_y1_st = P.op("gpsimd", lambda e: e.dma_start(
                    out=tap_ap["y1T"].rearrange("(c p) n -> p c n", p=128),
                    in_=y1T.rearrange("p (c n) -> p c n", c=32)), inc="sST", k=16)

            P.wait("vector", "sPE", stats_pe_done)
            P.op("vector", lambda e: e.tensor_copy(s1row[:, 0:512], st_ps[0:1, 0, :]))
            P.op("vector", lambda e: e.tensor_copy(s1row[:, 512:1024], st_ps[0:1, 1, :]))
            P.op("vector", lambda e: e.tensor_copy(s2row[:, 0:512], st_ps[0:1, 2, :]))
            dv_here = P.op("vector", lambda e: e.tensor_copy(
                s2row[:, 512:1024], st_ps[0:1, 3, :]), inc="sDV")
            P.wait("vector", "sDV", dv_here)
            P.op("vector", lambda e: e.tensor_scalar_mul(
                s1row[:, :], s1row[:, :], 1.0 / H), inc="sDV")
            P.wait("vector", "sDV", P.cnt["sDV"])
            P.op("vector", lambda e: e.tensor_mul(
                t1row[:, :], s1row[:, :], s1row[:, :]), inc="sDV")
            P.op("vector", lambda e: e.tensor_scalar_mul(
                s2row[:, :], s2row[:, :], 1.0 / H), inc="sDV")
            P.wait("vector", "sDV", P.cnt["sDV"])
            P.op("vector", lambda e: e.tensor_sub(
                s2row[:, :], s2row[:, :], t1row[:, :]), inc="sDV")
            P.wait("vector", "sDV", P.cnt["sDV"])
            dv_here = P.op("vector", lambda e: e.tensor_scalar_add(
                s2row[:, :], s2row[:, :], EPS), inc="sDV")
            P.wait("scalar", "sDV", dv_here)
            ev_here = P.op("scalar", lambda e: e.activation(
                t1row[:, :], s2row[:, :], AF.Sqrt), inc="sEV")
            P.wait("vector", "sEV", ev_here)
            rows_done = P.op("vector", lambda e: e.reciprocal(
                t1row[:, :], t1row[:, :]), inc="sDV")

            P.wait("tensor", "sDV", rows_done)
            for b in range(4):
                P.wait("tensor", "sDV", opb_ev.get(b))
            pe_here = None
            for t in range(2):
                P.op("tensor", lambda e, t=t: e.matmul(
                    op_ps[:, t, :], onesrf[0:1, :], s1row[0:1, t * 512:(t + 1) * 512],
                    start=True, stop=True))
                pe_here = P.op("tensor", lambda e, t=t: e.matmul(
                    op_ps[:, 2 + t, :], onesrf[0:1, :],
                    t1row[0:1, t * 512:(t + 1) * 512],
                    start=True, stop=True), inc="sPE", drain=(t == 1))
            P.wait("vector", "sPE", pe_here)
            for t in range(2):
                P.op("vector", lambda e, t=t: e.tensor_copy(
                    m1e[:, t * 512:(t + 1) * 512], op_ps[:, t, :]))
                P.op("vector", lambda e, t=t: e.tensor_copy(
                    r1e[:, t * 512:(t + 1) * 512], op_ps[:, 2 + t, :]), inc="sDV")
            bcast_done = P.cnt["sDV"]

            P.wait("vector", "sDV", bcast_done)
            P.wait("gpsimd", "sDV", bcast_done)
            if tap_y1_st is not None:
                P.wait("vector", "sST", tap_y1_st)
                P.wait("gpsimd", "sST", tap_y1_st)
            for h in range(DCH):
                if h % 2 == 0:
                    dv_here = P.op("vector", lambda e, h=h: e.tensor_sub(
                        tmpc[:, :], y1T[:, h * 1024:(h + 1) * 1024], m1e[:, :]),
                        inc="sDV")
                    P.wait("vector", "sDV", dv_here)
                    dv_here = P.op("vector", lambda e, h=h: e.tensor_mul(
                        y1T[:, h * 1024:(h + 1) * 1024], tmpc[:, :], r1e[:, :]),
                        inc="sDV")
                    P.wait("vector", "sDV", dv_here)
                else:
                    pl_here = P.op("gpsimd", lambda e, h=h: e.tensor_sub(
                        tmpc2[:, :], y1T[:, h * 1024:(h + 1) * 1024], m1e[:, :]),
                        inc="sPL")
                    P.wait("gpsimd", "sPL", pl_here)
                    pl_here = P.op("gpsimd", lambda e, h=h: e.tensor_mul(
                        y1T[:, h * 1024:(h + 1) * 1024], tmpc2[:, :], r1e[:, :]),
                        inc="sPL")
                    P.wait("gpsimd", "sPL", pl_here)
            ln1_done = P.cnt["sDV"]

            if "ln1T" in tap_ap:
                P.wait("gpsimd", "sDV", ln1_done)
                P.wait("gpsimd", "sPL", P.cnt["sPL"])
                P.op("gpsimd", lambda e: e.dma_start(
                    out=tap_ap["ln1T"].rearrange("(c p) n -> p c n", p=128),
                    in_=y1T.rearrange("p (c n) -> p c n", c=32)), inc="sST", k=16)

            if upto == "C":
                P.wait("gpsimd", "sST", P.cnt["sST"])
            P.barrier()

        sC.close()
        if upto == "C":
            return nc
        ln1T = y1T

        y2 = SL1[:, :]
        sD = ExitStack()
        wstr = sD.enter_context(nc.sbuf_tensor("wstr", [128, 2, 4096], BF16))
        bbb = sD.enter_context(nc.sbuf_tensor("bbb2", [1, H], BF16))
        BTb = sD.enter_context(nc.sbuf_tensor("BTb", [128, 4096], BF16))
        hwT = sD.enter_context(nc.sbuf_tensor("hwT", [128, 1024], BF16))
        re_sb = sD.enter_context(nc.sbuf_tensor("re_sb", [128, 1024], F32))
        S1c = sD.enter_context(nc.sbuf_tensor("S1c", [128, 64], F32))
        S2c = sD.enter_context(nc.sbuf_tensor("S2c", [128, 64], F32))
        sqscr = sD.enter_context(nc.sbuf_tensor("sqscr", [128, 512], F32))
        lnt = sD.enter_context(nc.sbuf_tensor("lnt", [128, 6], F32))
        ffn_ps = sD.enter_context(nc.psum_tensor("ffn_ps", [128, 8, 512], F32))
        if True:

            wstr_loaded = {}
            half_pe = {}

            def load_wstr(idx):
                ot, qq = idx // 4, idx % 4
                s = idx % 2
                P.wait("gpsimd", "sPE", half_pe.get(idx - 2))
                P.op("gpsimd", lambda e, ot=ot, qq=qq, s=s: e.dma_start(
                    out=wstr[:, s, :].rearrange("p (c f) -> p c f", c=8),
                    in_=d_WbT[qq * 1024:(qq + 1) * 1024, ot * 512:(ot + 1) * 512]
                    .rearrange("(c p) f -> p c f", p=128)), inc=f"sW{s}", k=16)
                wstr_loaded[idx] = (f"sW{s}", P.cnt[f"sW{s}"])

            load_wstr(0)
            load_wstr(1)

            sG = ExitStack()
            A2b = sG.enter_context(nc.sbuf_tensor("A2b", [128, 4096], BF16))
            lgt = sG.enter_context(nc.sbuf_tensor("lgt", [128, 64], F32))
            rte = sG.enter_context(nc.sbuf_tensor("rte", [128, 64], F32))
            rteT = sG.enter_context(nc.sbuf_tensor("rteT", [8, 1024], F32))
            gm1 = sG.enter_context(nc.sbuf_tensor("gm1", [128, 1], F32))
            gm2 = sG.enter_context(nc.sbuf_tensor("gm2", [128, 1], F32))
            gmsk = sG.enter_context(nc.sbuf_tensor("gmsk", [128, 8], F32))
            gtmp = sG.enter_context(nc.sbuf_tensor("gtmp", [128, 8], F32))
            gsum = sG.enter_context(nc.sbuf_tensor("gsum", [128, 1], F32))
            if True:

                P.op("gpsimd", lambda e: e.dma_start(out=bbb[:, :], in_=d_bb),
                     inc="sW", k=16)
                bbb_w = P.cnt["sW"]
                P.op("gpsimd", lambda e: e.dma_start(
                    out=A2b.rearrange("p (c f) -> p c f", c=32),
                    in_=d_A2.rearrange("(c p) f -> p c f", p=128)), inc="sW", k=16)
                lora_w = P.op("gpsimd", lambda e: e.dma_start(
                    out=BTb[:, :], in_=d_BT), inc="sW", k=16)

                gate_bank = {}
                for ncg in range(8):
                    pb = ncg % 2
                    P.wait("tensor", "sDV", gate_bank.get(pb))
                    pe_here = None
                    for d in range(DCH):
                        pe_here = P.op("tensor", lambda e, d=d, ncg=ncg, pb=pb:
                                       e.matmul(
                                           ffn_ps[:, pb, 0:8],
                                           ln1T[:, d * 1024 + ncg * 128:
                                                d * 1024 + ncg * 128 + 128],
                                           WgTb[:, d * 8:(d + 1) * 8],
                                           start=(d == 0), stop=(d == DCH - 1)),
                                       inc="sPE" if d == DCH - 1 else None,
                                       drain=(d == DCH - 1))
                    P.wait("vector", "sPE", pe_here)
                    gate_bank[pb] = P.op(
                        "vector", lambda e, ncg=ncg, pb=pb: e.tensor_copy(
                            lgt[:, ncg * 8:(ncg + 1) * 8], ffn_ps[:, pb, 0:8]),
                        inc="sDV")
                if upto == "G1":
                    if "route" in tap_ap:
                        P.wait("gpsimd", "sDV", P.cnt["sDV"])
                        P.op("gpsimd", lambda e: e.dma_start(
                            out=tap_ap["route"].rearrange("(c p) f -> p c f", p=128),
                            in_=lgt.rearrange("p (c f) -> p c f", c=8)),
                            inc="sST", k=16)
                    P.wait("gpsimd", "sST", P.cnt["sST"])
                    P.barrier()
                    sG.close()
                    sD.close()
                    return nc

                P.wait("tensor", wstr_loaded[0][0], wstr_loaded[0][1])
                for ncg in range(4, 8):
                    for d in range(8):
                        P.op("tensor", lambda e, d=d, ncg=ncg: e.matmul(
                            ffn_ps[:, ncg, :],
                            ln1T[:, d * 1024 + ncg * 128:
                                 d * 1024 + ncg * 128 + 128],
                            wstr[:, 0, d * 512:(d + 1) * 512],
                            start=(d == 0), stop=False))

                for ncg in range(8):
                    P.wait("vector", "sDV", P.cnt["sDV"])
                    dv = P.op("vector", lambda e, ncg=ncg: e.reduce_max(
                        gm1[:, :], lgt[:, ncg * 8:(ncg + 1) * 8], axis=AX.X,
                        negate=True), inc="sDV")
                    P.wait("vector", "sDV", dv)
                    dv = P.op("vector", lambda e, ncg=ncg: e.tensor_scalar(
                        gmsk[:, :], lgt[:, ncg * 8:(ncg + 1) * 8],
                        gm1[:, 0:1], -1.0, ALU.add, ALU.mult), inc="sDV")
                    P.wait("vector", "sDV", dv)
                    dv = P.op("vector", lambda e: e.tensor_single_scalar(
                        gmsk[:, :], gmsk[:, :], 0.0, ALU.is_le), inc="sDV")
                    P.wait("vector", "sDV", dv)
                    dv = P.op("vector", lambda e, ncg=ncg: e.scalar_tensor_tensor(
                        gtmp[:, :], gmsk[:, :], -1e30,
                        lgt[:, ncg * 8:(ncg + 1) * 8], ALU.mult, ALU.add), inc="sDV")
                    P.wait("vector", "sDV", dv)
                    dv = P.op("vector", lambda e: e.reduce_max(
                        gm2[:, :], gtmp[:, :], axis=AX.X), inc="sDV")
                    P.wait("vector", "sDV", dv)
                    dv = P.op("vector", lambda e, ncg=ncg: e.tensor_scalar(
                        gmsk[:, :], lgt[:, ncg * 8:(ncg + 1) * 8],
                        gm2[:, 0:1], None, ALU.is_ge), inc="sDV")
                    P.wait("scalar", "sDV", dv)
                    ev = P.op("scalar", lambda e, ncg=ncg: e.activation(
                        gtmp[:, :], lgt[:, ncg * 8:(ncg + 1) * 8], AF.Exp,
                        bias=gm1[:, 0:1]), inc="sEV")
                    P.wait("vector", "sEV", ev)
                    dv = P.op("vector", lambda e: e.tensor_mul(
                        gtmp[:, :], gtmp[:, :], gmsk[:, :]), inc="sDV")
                    P.wait("vector", "sDV", dv)
                    dv = P.op("vector", lambda e: e.reduce_sum(
                        gsum[:, :], gtmp[:, :], axis=AX.X), inc="sDV")
                    P.wait("vector", "sDV", dv)
                    dv = P.op("vector", lambda e: e.reciprocal(
                        gsum[:, :], gsum[:, :]), inc="sDV")
                    P.wait("vector", "sDV", dv)
                    P.op("vector", lambda e, ncg=ncg: e.tensor_scalar_mul(
                        rte[:, ncg * 8:(ncg + 1) * 8], gtmp[:, :], gsum[:, 0:1]),
                        inc="sDV")
                route_done = P.cnt["sDV"]

                if "route" in tap_ap:
                    P.wait("gpsimd", "sDV", route_done)
                    P.op("gpsimd", lambda e: e.dma_start(
                        out=tap_ap["route"].rearrange("(c p) f -> p c f", p=128),
                        in_=rte.rearrange("p (c f) -> p c f", c=8)), inc="sST", k=16)

                if upto == "G2":
                    P.wait("gpsimd", "sST", P.cnt["sST"])
                    P.barrier()
                    sG.close()
                    sD.close()
                    return nc

                P.wait("tensor", "sDV", route_done)
                for ncg in range(8):
                    pb = ncg % 2
                    P.wait("tensor", "sDV", gate_bank.get(pb))
                    pe_here = P.op("tensor", lambda e, ncg=ncg, pb=pb: e.transpose(
                        ffn_ps[0:8, pb, 0:128],
                        rte[:, ncg * 8:(ncg + 1) * 8], identf[:, :]),
                        inc="sPE", drain=True)
                    P.wait("vector", "sPE", pe_here)
                    gate_bank[pb] = P.op(
                        "vector", lambda e, ncg=ncg, pb=pb: e.tensor_copy(
                            rteT[0:8, ncg * 128:(ncg + 1) * 128],
                            ffn_ps[0:8, pb, 0:128]), inc="sDV")
                rteT_done = P.cnt["sDV"]

                if upto == "G3":
                    P.wait("gpsimd", "sST", P.cnt["sST"])
                    P.barrier()
                    sG.close()
                    sD.close()
                    return nc

                P.wait("tensor", "sDV", rteT_done)
                re_done = None
                re_evt = {}
                for t in range(2):
                    pe_here = P.op("tensor", lambda e, t=t: e.matmul(
                        ffn_ps[:, 2 + t, :], EXP8b[:, :],
                        rteT[0:8, t * 512:(t + 1) * 512],
                        start=True, stop=True), inc="sPE", drain=True)
                    P.wait("vector", "sPE", pe_here)
                    re_done = P.op("vector", lambda e, t=t: e.tensor_copy(
                        re_sb[:, t * 512:(t + 1) * 512], ffn_ps[:, 2 + t, :]),
                        inc="sDV")
                    re_evt[t] = re_done
                if upto == "G4":
                    P.wait("gpsimd", "sDV", P.cnt["sDV"])
                    P.wait("gpsimd", "sST", P.cnt["sST"])
                    P.barrier()
                    sG.close()
                    sD.close()
                    return nc

                P.wait("tensor", "sW", lora_w)
                for t in range(2):
                    P.wait("tensor", "sDV", re_evt[t])
                    pe_here = None
                    for d in range(DCH):
                        pe_here = P.op("tensor", lambda e, d=d, t=t: e.matmul(
                            ffn_ps[:, 2 + t, :], A2b[:, d * 128:(d + 1) * 128],
                            ln1T[:, d * 1024 + t * 512: d * 1024 + t * 512 + 512],
                            start=(d == 0), stop=(d == DCH - 1)),
                            inc="sPE" if d == DCH - 1 else None,
                            drain=(d == DCH - 1))
                    P.wait("vector", "sPE", pe_here)
                    P.wait("vector", "sDV", re_done)
                    P.op("vector", lambda e, t=t: e.tensor_mul(
                        hwT[:, t * 512:(t + 1) * 512], ffn_ps[:, 2 + t, :],
                        re_sb[:, t * 512:(t + 1) * 512]), inc="sDV")
                hw_done = P.cnt["sDV"]
                if "hwT" in tap_ap:
                    P.wait("gpsimd", "sDV", hw_done)
                    P.op("gpsimd", lambda e: e.dma_start(
                        out=tap_ap["hwT"].rearrange("(c p) n -> p c n", p=128),
                        in_=hwT.rearrange("p (c n) -> p c n", c=1)), inc="sST", k=16)

            sG.close()

            if upto == "G":
                P.wait("gpsimd", "sST", P.cnt["sST"])
                P.barrier()
                sD.close()
                return nc

            evac_dv = {}
            sq_last = None
            for ot in range(8):
                for qq in range(4):
                    idx = ot * 4 + qq
                    slot = idx % 2
                    P.wait("tensor", wstr_loaded[idx][0], wstr_loaded[idx][1])
                    if ot == 0 and qq == 0:
                        P.wait("tensor", "sDV", rteT_done)
                        P.wait("tensor", "sDV", hw_done)
                    if qq == 0 and ot > 0:
                        for ncg in range(8):
                            P.wait("tensor", "sDV", evac_dv.get(ncg))
                    for ncg in range(8):
                        if ot == 0 and qq == 0 and ncg >= 4:
                            continue
                        for d in range(8):
                            dg = qq * 8 + d
                            last = (d == 7 and
                                    ncg == (3 if (ot == 0 and qq == 0) else 7))
                            half_pe[idx] = P.op(
                                "tensor",
                                lambda e, s=slot, d=d, dg=dg, ncg=ncg: e.matmul(
                                    ffn_ps[:, ncg, :],
                                    ln1T[:, dg * 1024 + ncg * 128:
                                         dg * 1024 + ncg * 128 + 128],
                                    wstr[:, s, d * 512:(d + 1) * 512],
                                    start=(dg == 0), stop=False),
                                inc="sPE" if last else None)
                    if idx + 2 < 32:
                        load_wstr(idx + 2)
                for ncg in range(8):
                    P.op("tensor", lambda e, ncg=ncg, ot=ot: e.matmul(
                        ffn_ps[:, ncg, :], hwT[:, ncg * 128:(ncg + 1) * 128],
                        BTb[:, ot * 512:(ot + 1) * 512], start=False, stop=False))
                    if ot == 0 and ncg == 0:
                        P.wait("tensor", "sW", bbb_w)
                    P.op("tensor", lambda e, ncg=ncg, ot=ot: e.matmul(
                        ffn_ps[:, ncg, :], onesr[0:1, :],
                        bbb[0:1, ot * 512:(ot + 1) * 512], start=False, stop=False))
                    pe_here = None
                    for j in range(4):
                        pe_here = P.op("tensor", lambda e, ncg=ncg, ot=ot, j=j: e.matmul(
                            ffn_ps[:, ncg, j * 128:(j + 1) * 128],
                            ln1T[:, (ot * 4 + j) * 1024 + ncg * 128:
                                 (ot * 4 + j) * 1024 + ncg * 128 + 128],
                            identb[:, :], start=False, stop=(j == 3)),
                            inc="sPE" if j == 3 else None, drain=(j == 3))
                    P.wait("scalar", "sPE", pe_here)
                    if ot == 7 and ncg == 7:
                        P.delay("scalar", dly)
                    sq_last = P.op("scalar", lambda e, ncg=ncg, ot=ot: e.activation(
                        sqscr[:, :], ffn_ps[:, ncg, :], AF.Square,
                        accum_out=S2c[:, ncg * 8 + ot: ncg * 8 + ot + 1]), inc="sEV")
                    P.wait("vector", "sEV", sq_last)
                    evac_dv[ncg] = P.op(
                        "vector", lambda e, ncg=ncg, ot=ot: e.tensor_scalar(
                            y2[:, ncg * 4096 + ot * 512: ncg * 4096 + ot * 512 + 512],
                            ffn_ps[:, ncg, :], 1.0, 0.0, ALU.mult, ALU.add,
                            accum_out=S1c[:, ncg * 8 + ot: ncg * 8 + ot + 1]),
                        inc="sDV")
            ffn_done = P.cnt["sDV"]

            if "y2" in tap_ap:
                P.wait("gpsimd", "sDV", ffn_done)
                P.op("gpsimd", lambda e: e.dma_start(
                    out=tap_ap["y2"].rearrange("(c p) f -> p c f", p=128),
                    in_=y2.rearrange("p (c f) -> p c f", c=8)), inc="sST", k=16)

            outst = sD.enter_context(nc.sbuf_tensor("outst", [128, 2, 1024], F32))
            if True:
                st_done = {}
                P.wait("vector", "sDV", ffn_done)
                P.wait("vector", "sEV", sq_last)
                for ncg in range(8):
                    P.op("vector", lambda e, ncg=ncg: e.reduce_sum(
                        lnt[:, 0:1], S1c[:, ncg * 8:(ncg + 1) * 8], axis=AX.X),
                        inc="sDV")
                    dv = P.op("vector", lambda e, ncg=ncg: e.reduce_sum(
                        lnt[:, 1:2], S2c[:, ncg * 8:(ncg + 1) * 8], axis=AX.X),
                        inc="sDV")
                    P.wait("vector", "sDV", dv)
                    P.op("vector", lambda e: e.tensor_scalar_mul(
                        lnt[:, 0:1], lnt[:, 0:1], 1.0 / H), inc="sDV")
                    dv = P.op("vector", lambda e: e.tensor_scalar_mul(
                        lnt[:, 1:2], lnt[:, 1:2], 1.0 / H), inc="sDV")
                    P.wait("vector", "sDV", dv)
                    dv = P.op("vector", lambda e: e.tensor_mul(
                        lnt[:, 2:3], lnt[:, 0:1], lnt[:, 0:1]), inc="sDV")
                    P.wait("vector", "sDV", dv)
                    dv = P.op("vector", lambda e: e.tensor_sub(
                        lnt[:, 3:4], lnt[:, 1:2], lnt[:, 2:3]), inc="sDV")
                    P.wait("vector", "sDV", dv)
                    dv = P.op("vector", lambda e: e.tensor_scalar_add(
                        lnt[:, 3:4], lnt[:, 3:4], EPS), inc="sDV")
                    P.wait("scalar", "sDV", dv)
                    ev = P.op("scalar", lambda e: e.activation(
                        lnt[:, 4:5], lnt[:, 3:4], AF.Sqrt), inc="sEV")
                    P.wait("vector", "sEV", ev)
                    dv = P.op("vector", lambda e: e.reciprocal(
                        lnt[:, 5:6], lnt[:, 4:5]), inc="sDV")
                    P.wait("vector", "sDV", dv)
                    for q4 in range(4):
                        s = q4 % 2
                        P.wait("vector", f"sSO{s}", st_done.get(s))
                        dv = P.op("vector", lambda e, ncg=ncg, q4=q4, s=s:
                                  e.tensor_scalar(
                                      outst[:, s, :],
                                      y2[:, ncg * 4096 + q4 * 1024:
                                         ncg * 4096 + q4 * 1024 + 1024],
                                      lnt[:, 0:1], lnt[:, 5:6],
                                      ALU.subtract, ALU.mult), inc="sDV")
                        P.wait("sync", "sDV", dv)
                        st_done[s] = P.op(
                            "sync", lambda e, ncg=ncg, q4=q4, s=s: e.dma_start(
                                out=d_out[ncg * 128:(ncg + 1) * 128,
                                          q4 * 1024:(q4 + 1) * 1024],
                                in_=outst[:, s, :]), inc=f"sSO{s}", k=16)

                P.wait("gpsimd", "sSO0", P.cnt["sSO0"])
                P.wait("gpsimd", "sSO1", P.cnt["sSO1"])
                P.wait("gpsimd", "sST", P.cnt["sST"])
                P.barrier()
        sD.close()

    return nc


def _prep_inputs(L0_attn_x, L0_attn_Wqkv, L0_attn_bqkv, L0_attn_Wo, L0_attn_bo,
                 L0_ffn_W_base, L0_ffn_b_base, L0_ffn_W_gate, L0_ffn_A, L0_ffn_B):
    bf = ml_dtypes.bfloat16
    f32 = np.float32
    com = {
        "Wqkv": np.ascontiguousarray(np.asarray(L0_attn_Wqkv, dtype=f32)),
        "bqkv_pc": np.ascontiguousarray(np.asarray(L0_attn_bqkv, dtype=f32)
                                        .reshape(24, 128).T),
        "bv_row": np.ascontiguousarray(np.asarray(L0_attn_bqkv, dtype=f32)
                                       [2048:3072].reshape(1, DH)),
        "Wo": np.ascontiguousarray(np.asarray(L0_attn_Wo, dtype=f32)),
        "bo_pc": np.ascontiguousarray(np.asarray(L0_attn_bo, dtype=f32)
                                      .reshape(32, 128).T),
        "WbT": np.ascontiguousarray(np.asarray(L0_ffn_W_base, dtype=f32).T),
        "bb_row": np.ascontiguousarray(np.asarray(L0_ffn_b_base, dtype=f32)
                                       .reshape(1, H)),
        "WgT": np.ascontiguousarray(np.asarray(L0_ffn_W_gate, dtype=f32).T),
        "A2": np.ascontiguousarray(np.asarray(L0_ffn_A, dtype=f32)
                                   .transpose(2, 0, 1).reshape(H, 128)),
        "BT": np.ascontiguousarray(np.asarray(L0_ffn_B, dtype=f32)
                                   .transpose(0, 2, 1).reshape(128, H)),
        "EXP8": np.kron(np.eye(8, dtype=f32), np.ones((1, 16), dtype=f32)),
        "identb": np.eye(128, dtype=bf),
        "identf": np.eye(128, dtype=f32),
        "onescol": np.ones((128, 1), dtype=bf),
        "onesrow": np.ones((1, 128), dtype=bf),
        "onesrow512": np.ones((1, 512), dtype=bf),
        "bo_row": np.ascontiguousarray(np.asarray(L0_attn_bo, dtype=f32)
                                       .reshape(1, H)),
        "onesrowf": np.ones((1, 128), dtype=f32),
    }
    x = np.asarray(L0_attn_x, dtype=f32)
    in_maps = []
    for c in range(8):
        b, q0 = c // 4, (c % 4) * NT
        m = dict(com)
        m["xT"] = np.ascontiguousarray(x[b].T[:, q0:q0 + NT])
        g, my = c // 4, c % 4
        off = [0] * 4
        for j in (0, 1, 2, 3):
            rank = 4 * g + (my + j) % 4
            off[j] = rank * 2 * DH * NT
        m["kvoff"] = np.asarray([off], dtype=np.int32)
        in_maps.append(m)
    return in_maps


def kernel(**inputs):
    in_maps = _prep_inputs(**inputs)
    nc = build_nc(taps=DEBUG_TAPS, upto=UPTO)
    res = run_bass_kernel_spmd(nc, in_maps, core_ids=list(range(8)))
    out = np.zeros((2, S, H), dtype=np.float32)
    if UPTO == "D":
        for c in range(8):
            b, q0 = c // 4, (c % 4) * NT
            out[b, q0:q0 + NT, :] = np.asarray(res.results[c]["out"])
    if DEBUG_TAPS:
        kernel.last_results = res
    return out


# version 07
# speedup vs baseline: 1.0420x; 1.0420x over previous
from contextlib import ExitStack

import numpy as np
import ml_dtypes

import concourse.bass as bass
import concourse.mybir as mybir
from concourse.bass_utils import run_bass_kernel_spmd

F32 = mybir.dt.float32
BF16 = mybir.dt.bfloat16
AF = mybir.ActivationFunctionType
ALU = mybir.AluOpType
AX = mybir.AxisListType

H = 4096
DH = 1024
NT = 1024
S = 4096
NCH = NT // 128
DCH = H // 128
EPS = 1e-5
GROUPS = [[0, 1, 2, 3], [4, 5, 6, 7]]

DEBUG_TAPS = ()
UPTO = "D"

ENGS = ("tensor", "vector", "scalar", "gpsimd", "sync")


class Prog:

    def __init__(self, nc, stack):
        self.nc = nc
        self.stack = stack
        self.streams = {e: [] for e in ENGS}
        self.cnt = {}
        self.hdl = {}
        self.nblk = 0

    def sem(self, name):
        self.hdl[name] = self.stack.enter_context(self.nc.semaphore(name))
        self.cnt[name] = 0

    def op(self, eng, builder, inc=None, k=1, drain=False):
        if inc is not None and drain:
            assert eng == "tensor"
            self.cnt[inc] += k
            h = self.hdl[inc]

            def emit(e, b=builder, h=h, k=k):
                b(e)
                e.drain().then_inc(h, k)
            self.streams[eng].append(emit)
            return self.cnt[inc]
        if inc is None:
            self.streams[eng].append(lambda e, b=builder: b(e))
            return None
        self.cnt[inc] += k
        h = self.hdl[inc]
        self.streams[eng].append(lambda e, b=builder, h=h, k=k: b(e).then_inc(h, k))
        return self.cnt[inc]

    def delay(self, eng, scratch):
        if eng == "vector":
            self.streams[eng].append(
                lambda e, s=scratch: e.tensor_copy(s[0:1, 0:1], s[0:1, 1:2]))
        else:
            self.streams[eng].append(
                lambda e, s=scratch: e.copy(s[0:1, 0:1], s[0:1, 1:2]))

    def wait(self, eng, sname, val):
        if val is None or val <= 0:
            return
        h = self.hdl[sname]
        self.streams[eng].append(lambda e, h=h, v=val: e.wait_ge(h, v))

    def barrier(self):
        self.nblk += 1
        streams, self.streams = self.streams, {e: [] for e in ENGS}
        with self.nc.Block(f"ph{self.nblk}") as blk:
            for eng in ENGS:
                fns = streams[eng]
                if not fns:
                    continue
                getattr(blk, eng)(lambda e, fns=fns: [f(e) for f in fns])


def build_nc(taps=(), upto="D"):
    nc = bass.Bass("TRN2", target_bir_lowering=False)

    d_xT = nc.dram_tensor("xT", [H, NT], F32, kind="ExternalInput").ap()
    d_Wqkv = nc.dram_tensor("Wqkv", [H, 3 * DH], F32, kind="ExternalInput").ap()
    d_bqkv = nc.dram_tensor("bqkv_pc", [128, 24], F32, kind="ExternalInput").ap()
    d_bv = nc.dram_tensor("bv_row", [1, DH], F32, kind="ExternalInput").ap()
    d_Wo = nc.dram_tensor("Wo", [DH, H], F32, kind="ExternalInput").ap()
    d_bo = nc.dram_tensor("bo_pc", [128, 32], F32, kind="ExternalInput").ap()
    d_WbT = nc.dram_tensor("WbT", [H, H], F32, kind="ExternalInput").ap()
    d_bb = nc.dram_tensor("bb_row", [1, H], F32, kind="ExternalInput").ap()
    d_WgT = nc.dram_tensor("WgT", [H, 8], F32, kind="ExternalInput").ap()
    d_A2 = nc.dram_tensor("A2", [H, 128], F32, kind="ExternalInput").ap()
    d_BT = nc.dram_tensor("BT", [128, H], F32, kind="ExternalInput").ap()
    d_EXP8 = nc.dram_tensor("EXP8", [8, 128], F32, kind="ExternalInput").ap()
    d_idb = nc.dram_tensor("identb", [128, 128], BF16, kind="ExternalInput").ap()
    d_idf = nc.dram_tensor("identf", [128, 128], F32, kind="ExternalInput").ap()
    d_onc = nc.dram_tensor("onescol", [128, 1], BF16, kind="ExternalInput").ap()
    d_onr = nc.dram_tensor("onesrow", [1, 128], BF16, kind="ExternalInput").ap()
    d_onr512 = nc.dram_tensor("onesrow512", [1, 512], BF16, kind="ExternalInput").ap()
    d_bor = nc.dram_tensor("bo_row", [1, H], F32, kind="ExternalInput").ap()
    d_onrf = nc.dram_tensor("onesrowf", [1, 128], F32, kind="ExternalInput").ap()
    d_out = nc.dram_tensor("out", [NT, H], F32, kind="ExternalOutput").ap()

    cc_in = nc.dram_tensor("cc_in", [2 * DH, NT], BF16, kind="Internal")
    cc_out = nc.dram_tensor("cc_out", [8 * 2 * DH, NT], BF16, kind="Internal",
                            addr_space="Shared")
    d_kvoff = nc.dram_tensor("kvoff", [1, 4], mybir.dt.int32,
                             kind="ExternalInput").ap()

    tap_shapes = {
        "Kown": [DH, NT], "Vown": [NT, DH], "Qb": [DH, NT],
        "attn": [NT, DH], "y1T": [H, NT], "ln1T": [H, NT],
        "route": [NT, 8], "hwT": [128, NT], "y2": [NT, H],
    }
    tap_ap = {t: nc.dram_tensor("tap_" + t, tap_shapes[t], F32,
                                kind="ExternalOutput").ap() for t in taps}

    with ExitStack() as stack:
        ent = stack.enter_context
        P = Prog(nc, stack)
        for s in ("sX", "sW", "sSY", "sPE", "sEV", "sDV", "sCC", "sST", "sPL",
                  "sW0", "sW1", "sWq0", "sWq1", "sXF0", "sXF1", "sKV0", "sKV1",
                  "sSO0", "sSO1"):
            P.sem(s)

        SL1 = ent(nc.sbuf_tensor("SL1", [128, 32768], BF16))
        SL2 = ent(nc.sbuf_tensor("SL2", [128, 32768], BF16))
        bq_pc = ent(nc.sbuf_tensor("bq_pc", [128, 24], F32))
        bo_pc = ent(nc.sbuf_tensor("sb_bo_pc", [128, 32], F32))
        identb = ent(nc.sbuf_tensor("sb_identb", [128, 128], BF16))
        identf = ent(nc.sbuf_tensor("sb_identf", [128, 128], F32))
        onesc = ent(nc.sbuf_tensor("onesc", [128, 1], BF16))
        onesr = ent(nc.sbuf_tensor("onesr", [1, 128], BF16))
        onesr512 = ent(nc.sbuf_tensor("onesr512", [1, 512], BF16))
        onesrf = ent(nc.sbuf_tensor("onesrf", [1, 128], F32))
        WgTb = ent(nc.sbuf_tensor("WgTb", [128, 32 * 8], BF16))
        EXP8b = ent(nc.sbuf_tensor("EXP8b", [8, 128], F32))
        dly = ent(nc.sbuf_tensor("dly", [1, 8], F32))
        kvoffb = ent(nc.sbuf_tensor("kvoffb", [1, 4], mybir.dt.int32))
        stackAB = ExitStack()
        Qb = stackAB.enter_context(nc.sbuf_tensor("Qb", [128, 8192], BF16))
        Kown = stackAB.enter_context(nc.sbuf_tensor("Kown", [128, 8192], BF16))
        rk = stackAB.enter_context(nc.gpsimd.register("rk"))
        rv = stackAB.enter_context(nc.gpsimd.register("rv"))

        xTb = SL1[:, :]
        wkq = [SL2[:, i * 4096:(i + 1) * 4096] for i in range(4)]
        wv = [SL2[:, 16384 + i * 8192: 16384 + (i + 1) * 8192] for i in range(2)]

        sA = ExitStack()
        arena = sA.enter_context(nc.psum_tensor("arena", [128, 8, 512], F32))
        Vown = sA.enter_context(nc.sbuf_tensor("Vown", [128, 8192], BF16))
        bvb = sA.enter_context(nc.sbuf_tensor("bvb", [1, DH], BF16))
        if True:
            P.op("gpsimd", lambda e: e.dma_start(
                out=xTb.rearrange("p (c n) -> p c n", c=32),
                in_=d_xT.rearrange("(c p) n -> p c n", p=128)), inc="sX", k=16)
            P.op("gpsimd", lambda e: e.dma_start(out=bvb[:, :], in_=d_bv),
                 inc="sX", k=16)
            P.op("gpsimd", lambda e: e.dma_start(
                out=WgTb.rearrange("p (c f) -> p c f", c=32),
                in_=d_WgT.rearrange("(c p) f -> p c f", p=128)), inc="sX", k=16)
            x_done = bv_done = P.cnt["sX"]

            for dst, src in ((bq_pc, d_bqkv), (bo_pc, d_bo), (identb, d_idb),
                             (identf, d_idf), (onesc, d_onc), (onesr, d_onr),
                             (onesrf, d_onrf), (EXP8b, d_EXP8),
                             (onesr512, d_onr512), (kvoffb, d_kvoff)):
                P.op("sync", lambda e, a=dst, b=src: e.dma_start(out=a[:, :], in_=b),
                     inc="sSY", k=16)
            sy_const = P.cnt["sSY"]

            pe_stripe = {}
            stripe_w = {}
            bank_ev = {}

            def load_stripe(seq, ci):
                slot = seq % 4
                prev = pe_stripe.get(seq - 4)
                if prev is not None:
                    P.wait("gpsimd", "sPE", prev)
                sem = f"sW{slot % 2}" if slot < 2 else f"sWq{slot - 2}"
                P.op("gpsimd", lambda e, ci=ci, s=slot: e.dma_start(
                    out=wkq[s].rearrange("p (c f) -> p c f", c=32),
                    in_=d_Wqkv[:, ci * 128:(ci + 1) * 128]
                    .rearrange("(c p) f -> p c f", p=128)), inc=sem, k=16)
                stripe_w[seq] = (sem, P.cnt[sem])

            def proj_ft(kind, idx, ci, dst, scale):
                seq = idx if kind == "K" else 8 + idx
                slot = seq % 4
                if seq not in stripe_w:
                    load_stripe(seq, ci)
                P.wait("tensor", stripe_w[seq][0], stripe_w[seq][1])
                if kind == "K" and idx == 0:
                    P.wait("tensor", "sX", x_done)
                if kind == "Q" and idx == 0:
                    P.wait("tensor", "sEV", kv_evac_total[0])
                for ntile in range(2):
                    bank = (2 * idx + ntile) % 4 + (4 if kind == "Q" else 0)
                    P.wait("tensor", "sEV", bank_ev.get(bank))
                    pe_here = None
                    for d in range(DCH):
                        pe_here = P.op(
                            "tensor", lambda e, s=slot, d=d, nt=ntile, b=bank: e.matmul(
                                arena[:, b, :],
                                wkq[s][:, d * 128:(d + 1) * 128],
                                xTb[:, d * 1024 + nt * 512: d * 1024 + nt * 512 + 512],
                                start=(d == 0), stop=(d == DCH - 1)),
                            inc="sPE" if d == DCH - 1 else None,
                            drain=(d == DCH - 1))
                    P.wait("vector", "sPE", pe_here)
                    if kind == "Q" and idx == 7 and ntile == 1:
                        P.delay("vector", dly)
                    P.wait("vector", "sSY", sy_const)
                    bias_ap = bq_pc[:, ci:ci + 1]
                    bank_ev[bank] = P.op(
                        "vector", lambda e, dst=dst, i=idx, nt=ntile, b=bank,
                        ba=bias_ap, sc=scale: e.tensor_scalar(
                            dst[:, i * 1024 + nt * 512: i * 1024 + nt * 512 + 512],
                            arena[:, b, :], ba, sc, ALU.add, ALU.mult), inc="sEV")
                pe_stripe[seq] = P.cnt["sPE"]

            kv_evac_total = [0]
            for i in range(8):
                proj_ft("K", i, 8 + i, Kown, 1.0)
            k_evac_total = P.cnt["sEV"]

            for dt_ in range(2):
                for hhalf in range(2):
                    vidx = 2 * dt_ + hhalf
                    slot = vidx % 2
                    prev = pe_stripe.get(("V", vidx - 2))
                    if prev is not None:
                        P.wait("gpsimd", "sPE", prev)
                    P.op("gpsimd", lambda e, dt_=dt_, h=hhalf, s=slot: e.dma_start(
                        out=wv[s].rearrange("p (c f) -> p c f", c=16),
                        in_=d_Wqkv[h * 2048:(h + 1) * 2048,
                                   2048 + dt_ * 512: 2048 + (dt_ + 1) * 512]
                        .rearrange("(c p) f -> p c f", p=128)), inc=f"sW{slot}", k=16)
                    P.wait("tensor", f"sW{slot}", P.cnt[f"sW{slot}"])
                    if vidx == 0:
                        P.wait("tensor", "sEV", k_evac_total)
                    if dt_ == 1 and hhalf == 0:
                        P.wait("tensor", "sEV", P.cnt["sEV"])
                    pe_here = None
                    for d in range(16):
                        dg = hhalf * 16 + d
                        for ncg in range(8):
                            last = (d == 15 and ncg == 7)
                            pe_here = P.op(
                                "tensor",
                                lambda e, s=slot, d=d, dg=dg, ncg=ncg: e.matmul(
                                    arena[:, ncg, :],
                                    xTb[:, dg * 1024 + ncg * 128:
                                        dg * 1024 + ncg * 128 + 128],
                                    wv[s][:, d * 512:(d + 1) * 512],
                                    start=(dg == 0), stop=False),
                                inc="sPE" if last else None)
                    pe_stripe[("V", vidx)] = pe_here
                P.wait("tensor", "sX", bv_done)
                for ncg in range(8):
                    pe_here = P.op("tensor", lambda e, ncg=ncg, dt_=dt_: e.matmul(
                        arena[:, ncg, :], onesr[0:1, :],
                        bvb[0:1, dt_ * 512:(dt_ + 1) * 512],
                        start=False, stop=True), inc="sPE", drain=True)
                    P.wait("scalar", "sPE", pe_here)
                    P.op("scalar", lambda e, ncg=ncg, dt_=dt_: e.activation(
                        Vown[:, ncg * 1024 + dt_ * 512: ncg * 1024 + dt_ * 512 + 512],
                        arena[:, ncg, :], AF.Copy), inc="sEV")
            kv_evac_total[0] = P.cnt["sEV"]

            for qi in range(4):
                load_stripe(8 + qi, qi)

            P.wait("gpsimd", "sEV", kv_evac_total[0])
            P.op("gpsimd", lambda e: e.dma_start(
                out=cc_in.ap()[0:DH, :].rearrange("(c p) n -> p c n", p=128),
                in_=Kown.rearrange("p (c n) -> p c n", c=8)), inc="sW", k=16)
            P.op("gpsimd", lambda e: e.dma_start(
                out=cc_in.ap()[DH:2 * DH, :].rearrange("(c p) n -> p c n", p=128),
                in_=Vown.rearrange("p (c n) -> p c n", c=8)), inc="sW", k=16)

            P.wait("gpsimd", "sW", P.cnt["sW"])
            P.op("gpsimd", lambda e: e.collective_compute(
                "AllGather", ALU.bypass, replica_groups=[list(range(8))],
                ins=[cc_in.ap().opt()], outs=[cc_out.ap().opt()]), inc="sCC")

            for i in range(8):
                proj_ft("Q", i, i, Qb, 1.0 / 32.0)

            if "Kown" in tap_ap:
                P.wait("gpsimd", "sEV", kv_evac_total[0])
                P.op("gpsimd", lambda e: e.dma_start(
                    out=tap_ap["Kown"].rearrange("(c p) n -> p c n", p=128),
                    in_=Kown.rearrange("p (c n) -> p c n", c=8)), inc="sST", k=16)
                P.op("gpsimd", lambda e: e.dma_start(
                    out=tap_ap["Vown"].rearrange("(c p) n -> p c n", p=128),
                    in_=Vown.rearrange("p (c n) -> p c n", c=8)), inc="sST", k=16)
            if "Qb" in tap_ap:
                P.wait("gpsimd", "sEV", P.cnt["sEV"])
                P.op("gpsimd", lambda e: e.dma_start(
                    out=tap_ap["Qb"].rearrange("(c p) n -> p c n", p=128),
                    in_=Qb.rearrange("p (c n) -> p c n", c=8)), inc="sST", k=16)

            if upto == "A":
                P.wait("gpsimd", "sST", P.cnt["sST"])
            P.barrier()

        sA.close()
        if upto == "A":
            stackAB.close()
            return nc

        attnTf = SL1[:, 0:8192]
        attn_bf = SL1[:, 8192:16384]
        Kblk = [SL2[:, i * 8192:(i + 1) * 8192] for i in range(2)]
        Vblk = [SL2[:, 16384 + i * 8192: 16384 + (i + 1) * 8192] for i in range(2)]

        sB = ExitStack()
        acc = sB.enter_context(nc.sbuf_tensor("acc", [128, 8192], F32))
        Pb = sB.enter_context(nc.sbuf_tensor("Pb", [128, 2, 1024], BF16))
        PT = sB.enter_context(nc.sbuf_tensor("PT", [128, 2, 1024], BF16))
        mrun = sB.enter_context(nc.sbuf_tensor("mrun", [128, 8], F32))
        srun = sB.enter_context(nc.sbuf_tensor("srun", [128, 8], F32))
        mb2 = sB.enter_context(nc.sbuf_tensor("mb2", [128, 2, 2], F32))
        mnew = sB.enter_context(nc.sbuf_tensor("mnew", [128, 2], F32))
        negm = sB.enter_context(nc.sbuf_tensor("negm", [128, 2], F32))
        alpha = sB.enter_context(nc.sbuf_tensor("alpha", [128, 2], F32))
        sblk = sB.enter_context(nc.sbuf_tensor("sblk", [128, 2, 2], F32))
        rsum = sB.enter_context(nc.sbuf_tensor("rsum", [128, 1], F32))
        att_ps = sB.enter_context(nc.psum_tensor("att_ps", [128, 6, 512], F32))
        pt_ps = sB.enter_context(nc.psum_tensor("pt_ps", [128, 2, 1024], BF16))
        if True:

            kv_loaded = {}
            pe_blk = {}

            kv_pattern = [[1024, 128], [131072, 8], [1, 1024]]

            def load_kv(r):
                s = r % 2
                P.op("gpsimd", lambda e, r=r: e.reg_load(rk, kvoffb[0:1, r:r + 1]))
                P.op("gpsimd", lambda e: e.reg_add(rv, rk, DH * NT))
                if r > 0:
                    P.op("gpsimd", lambda e, s=s: e.dma_start(
                        out=Kblk[s].rearrange("p (c n) -> p c n", c=8),
                        in_=bass.AP(cc_out, rk, kv_pattern)), inc=f"sKV{s}", k=16)
                P.op("gpsimd", lambda e, s=s: e.dma_start(
                    out=Vblk[s].rearrange("p (c n) -> p c n", c=8),
                    in_=bass.AP(cc_out, rv, kv_pattern)), inc=f"sKV{s}", k=16)
                kv_loaded[r] = (s, P.cnt[f"sKV{s}"])

            P.wait("gpsimd", "sSY", sy_const)
            P.wait("gpsimd", "sCC", 1)
            load_kv(0)

            dv_bank = {}
            pt_bank = {}
            pb_free = {}
            pe_sc = {}
            exp_ev = {}

            def emit_scores(r, qc):
                kb = kvbuf(r)[0]
                p = qc % 2
                for t in range(2):
                    bank = 2 * p + t
                    fb = dv_bank.get(bank)
                    if fb is not None:
                        P.wait("tensor", fb[0], fb[1])
                    pe_here = None
                    for m in range(8):
                        pe_here = P.op(
                            "tensor",
                            lambda e, kb=kb, m=m, qc=qc, t=t, b=bank: e.matmul(
                                att_ps[:, b, :],
                                Qb[:, m * 1024 + qc * 128:
                                   m * 1024 + qc * 128 + 128],
                                kb[:, m * 1024 + t * 512:
                                   m * 1024 + t * 512 + 512],
                                start=(m == 0), stop=(m == 7)),
                            inc="sPE" if m == 7 else None, drain=(m == 7))
                    pe_sc[(qc, t)] = pe_here

            def emit_soft(r, qc):
                p = qc % 2
                mx_done = None
                for t in range(2):
                    bank = 2 * p + t
                    P.wait("vector", "sPE", pe_sc[(qc, t)])
                    mx_done = P.op("vector", lambda e, p=p, t=t, b=bank: e.reduce_max(
                        mb2[:, p, t:t + 1], att_ps[:, b, :], axis=AX.X), inc="sDV")
                P.wait("vector", "sDV", mx_done)
                if r == 0:
                    P.op("vector", lambda e, qc=qc, p=p: e.reduce_max(
                        mrun[:, qc:qc + 1], mb2[:, p, :], axis=AX.X), inc="sDV")
                    P.wait("vector", "sDV", P.cnt["sDV"])
                    dv_here = P.op("vector", lambda e, qc=qc, p=p: e.tensor_scalar_mul(
                        negm[:, p:p + 1], mrun[:, qc:qc + 1], -1.0), inc="sDV")
                    P.wait("vector", "sDV", dv_here)
                else:
                    P.op("vector", lambda e, p=p: e.reduce_max(
                        mnew[:, p:p + 1], mb2[:, p, :], axis=AX.X), inc="sDV")
                    P.wait("vector", "sDV", P.cnt["sDV"])
                    P.op("vector", lambda e, qc=qc, p=p: e.tensor_max(
                        mnew[:, p:p + 1], mnew[:, p:p + 1], mrun[:, qc:qc + 1]),
                        inc="sDV")
                    P.wait("vector", "sDV", P.cnt["sDV"])
                    dv_here = P.op("vector", lambda e, p=p: e.tensor_scalar_mul(
                        negm[:, p:p + 1], mnew[:, p:p + 1], -1.0), inc="sDV")
                    P.wait("vector", "sDV", dv_here)
                    P.wait("scalar", "sDV", dv_here)
                    al_done = P.op("scalar", lambda e, qc=qc, p=p: e.activation(
                        alpha[:, p:p + 1], mrun[:, qc:qc + 1], AF.Exp,
                        bias=negm[:, p:p + 1]), inc="sEV")
                    P.wait("vector", "sEV", al_done)
                    dv_here = P.op("vector", lambda e, qc=qc, p=p: e.tensor_copy(
                        mrun[:, qc:qc + 1], mnew[:, p:p + 1]), inc="sDV")
                    P.wait("vector", "sDV", dv_here)
                P.wait("scalar", "sDV", P.cnt["sDV"])
                P.wait("scalar", "sPE", pb_free.get(p))
                for t in range(2):
                    bank = 2 * p + t
                    exp_ev[qc] = P.op("scalar", lambda e, p=p, t=t, b=bank:
                                      e.activation(
                                          Pb[:, p, t * 512:(t + 1) * 512],
                                          att_ps[:, b, :], AF.Exp,
                                          bias=negm[:, p:p + 1],
                                          accum_out=sblk[:, p, t:t + 1]),
                                      inc="sEV")
                    dv_bank[bank] = ("sEV", P.cnt["sEV"])
                P.wait("vector", "sEV", exp_ev[qc])
                if r == 0:
                    P.op("vector", lambda e, qc=qc, p=p: e.tensor_add(
                        srun[:, qc:qc + 1], sblk[:, p, 0:1], sblk[:, p, 1:2]),
                        inc="sDV")
                else:
                    dv = P.op("vector", lambda e, qc=qc, p=p: e.scalar_tensor_tensor(
                        srun[:, qc:qc + 1], srun[:, qc:qc + 1], alpha[:, p:p + 1],
                        sblk[:, p, 0:1], ALU.mult, ALU.add), inc="sDV")
                    P.wait("vector", "sDV", dv)
                    P.op("vector", lambda e, qc=qc, p=p: e.tensor_add(
                        srun[:, qc:qc + 1], srun[:, qc:qc + 1], sblk[:, p, 1:2]),
                        inc="sDV")

            def emit_tail(r, qc):
                vb = kvbuf(r)[1]
                p = qc % 2
                if r == 0 and qc == 0:
                    P.wait("tensor", f"sKV{kv_loaded[0][0]}", kv_loaded[0][1])
                P.wait("tensor", "sEV", exp_ev[qc])
                pt_copy = None
                for kc in range(8):
                    pb = kc % 2
                    P.wait("tensor", "sDV", pt_bank.get(pb))
                    pe_here = P.op(
                        "tensor", lambda e, kc=kc, pb=pb, p=p: e.transpose(
                            pt_ps[:, pb, 0:128], Pb[:, p, kc * 128:(kc + 1) * 128],
                            identb[:, :]),
                        inc="sPE", drain=True)
                    P.wait("vector", "sPE", pe_here)
                    pt_copy = P.op(
                        "vector", lambda e, kc=kc, pb=pb, p=p: e.tensor_copy(
                            PT[:, p, kc * 128:(kc + 1) * 128], pt_ps[:, pb, 0:128]),
                        inc="sDV")
                    pt_bank[pb] = pt_copy
                pb_free[p] = P.cnt["sPE"]
                P.wait("tensor", "sDV", pt_copy)
                for t in range(2):
                    bank = 4 + t
                    fb = dv_bank.get(bank)
                    if fb is not None:
                        P.wait("tensor", fb[0], fb[1])
                    pe_here = None
                    for kc in range(8):
                        pe_here = P.op(
                            "tensor",
                            lambda e, vb=vb, kc=kc, t=t, b=bank, p=p: e.matmul(
                                att_ps[:, b, :],
                                PT[:, p, kc * 128:(kc + 1) * 128],
                                vb[:, kc * 1024 + t * 512:
                                   kc * 1024 + t * 512 + 512],
                                start=(kc == 0), stop=(kc == 7)),
                            inc="sPE" if kc == 7 else None, drain=(kc == 7))
                    P.wait("vector", "sPE", pe_here)
                    if r == 0:
                        dv_bank[bank] = ("sDV", P.op(
                            "vector", lambda e, qc=qc, t=t, b=bank: e.tensor_copy(
                                acc[:, qc * 1024 + t * 512:
                                    qc * 1024 + t * 512 + 512],
                                att_ps[:, b, :]), inc="sDV"))
                    else:
                        dv = P.op(
                            "vector", lambda e, qc=qc, t=t, p=p: e.tensor_scalar_mul(
                                acc[:, qc * 1024 + t * 512:
                                    qc * 1024 + t * 512 + 512],
                                acc[:, qc * 1024 + t * 512:
                                    qc * 1024 + t * 512 + 512],
                                alpha[:, p:p + 1]), inc="sDV")
                        P.wait("vector", "sDV", dv)
                        dv_bank[bank] = ("sDV", P.op(
                            "vector", lambda e, qc=qc, t=t, b=bank: e.tensor_add(
                                acc[:, qc * 1024 + t * 512:
                                    qc * 1024 + t * 512 + 512],
                                acc[:, qc * 1024 + t * 512:
                                    qc * 1024 + t * 512 + 512],
                                att_ps[:, b, :]), inc="sDV"))

            def kvbuf(r):
                return (Kown, Vblk[0]) if r == 0 else (Kblk[r % 2], Vblk[r % 2])

            for r in range(4):
                if r + 1 < 4:
                    P.wait("gpsimd", "sPE", pe_blk.get(r - 1))
                    load_kv(r + 1)
                if r == 0:
                    emit_scores(0, 0)
                for qc in range(NCH):
                    emit_soft(r, qc)
                    if qc + 1 < NCH:
                        emit_scores(r, qc + 1)
                    elif r + 1 < 4:
                        P.wait("tensor", f"sKV{kv_loaded[r + 1][0]}",
                               kv_loaded[r + 1][1])
                        emit_scores(r + 1, 0)
                    emit_tail(r, qc)
                pe_blk[r] = P.cnt["sPE"]

            for qc in range(NCH):
                P.wait("vector", "sDV", P.cnt["sDV"])
                dv_here = P.op("vector", lambda e, qc=qc: e.reciprocal(
                    rsum[:, :], srun[:, qc:qc + 1]), inc="sDV")
                P.wait("vector", "sDV", dv_here)
                P.op("vector", lambda e, qc=qc: e.tensor_scalar_mul(
                    attn_bf[:, qc * 1024:(qc + 1) * 1024],
                    acc[:, qc * 1024:(qc + 1) * 1024], rsum[:, 0:1]), inc="sDV")
            attn_done = P.cnt["sDV"]
            P.wait("tensor", "sDV", attn_done)
            for qc in range(NCH):
                for m in range(8):
                    pb = m % 2
                    P.wait("tensor", "sDV", pt_bank.get(pb))
                    pe_here = P.op("tensor", lambda e, qc=qc, m=m, pb=pb: e.transpose(
                        pt_ps[:, pb, 0:128],
                        attn_bf[:, qc * 1024 + m * 128: qc * 1024 + m * 128 + 128],
                        identb[:, :]), inc="sPE", drain=True)
                    P.wait("vector", "sPE", pe_here)
                    if qc == NCH - 1 and m == 7:
                        P.delay("vector", dly)
                    pt_bank[pb] = P.op(
                        "vector", lambda e, qc=qc, m=m, pb=pb: e.tensor_copy(
                            attnTf[:, m * 1024 + qc * 128: m * 1024 + qc * 128 + 128],
                            pt_ps[:, pb, 0:128]), inc="sDV")

            if "attn" in tap_ap:
                P.wait("gpsimd", "sDV", attn_done)
                P.op("gpsimd", lambda e: e.dma_start(
                    out=tap_ap["attn"].rearrange("(c p) n -> p c n", p=128),
                    in_=attn_bf.rearrange("p (c n) -> p c n", c=8)), inc="sST", k=16)

            if upto == "B":
                P.wait("gpsimd", "sST", P.cnt["sST"])
            P.barrier()

        sB.close()
        stackAB.close()
        if upto == "B":
            return nc

        y1T = SL2[:, :]
        sC = ExitStack()
        wo_sb = sC.enter_context(nc.sbuf_tensor("wo_sb", [128, 2, 4096], BF16))
        bo_rowb = sC.enter_context(nc.sbuf_tensor("bo_rowb", [1, H], BF16))
        xf32 = sC.enter_context(nc.sbuf_tensor("xf32", [128, 2, 1024], F32))
        y1sq = sC.enter_context(nc.sbuf_tensor("y1sq", [128, 3, 1024], BF16))
        s1row = sC.enter_context(nc.sbuf_tensor("s1row", [1, 1024], F32))
        s2row = sC.enter_context(nc.sbuf_tensor("s2row", [1, 1024], F32))
        t1row = sC.enter_context(nc.sbuf_tensor("t1row", [1, 1024], F32))
        m1e = sC.enter_context(nc.sbuf_tensor("m1e", [128, 1024], F32))
        r1e = sC.enter_context(nc.sbuf_tensor("r1e", [128, 1024], F32))
        tmpc = sC.enter_context(nc.sbuf_tensor("tmpc", [128, 1024], F32))
        tmpc2 = sC.enter_context(nc.sbuf_tensor("tmpc2", [128, 1024], F32))
        op_ps = sC.enter_context(nc.psum_tensor("op_ps", [128, 4, 512], F32))
        st_ps = sC.enter_context(nc.psum_tensor("st_ps", [1, 4, 512], F32))
        if True:

            wo_loaded = {}
            xf_loaded = {}
            y1_dv = {}
            sq_ev = {}
            stat_pe = {}
            opb_ev = {}

            def load_wo(hb):
                s = hb % 2
                P.op("gpsimd", lambda e, hb=hb, s=s: e.dma_start(
                    out=wo_sb[:, s, :].rearrange("p (c f) -> p c f", c=8),
                    in_=d_Wo[:, hb * 512:(hb + 1) * 512]
                    .rearrange("(c p) f -> p c f", p=128)), inc=f"sW{s}", k=16)
                wo_loaded[hb] = (s, P.cnt[f"sW{s}"])

            def load_xf(h):
                s = h % 2
                if h - 2 >= 0:
                    P.wait("gpsimd", "sDV", y1_dv.get(h - 2))
                P.op("gpsimd", lambda e, h=h, s=s: e.dma_start(
                    out=xf32[:, s, :], in_=d_xT[h * 128:(h + 1) * 128, :]),
                    inc=f"sXF{s}", k=16)
                xf_loaded[h] = (s, P.cnt[f"sXF{s}"])

            P.op("gpsimd", lambda e: e.dma_start(out=bo_rowb[:, :], in_=d_bor),
                 inc="sW", k=16)
            load_wo(0)
            load_wo(1)
            load_xf(0)
            load_xf(1)

            def stats_for(h):
                P.wait("tensor", "sDV", y1_dv[h])
                for t in range(2):
                    P.op("tensor", lambda e, h=h, t=t: e.matmul(
                        st_ps[0:1, t, :], onesc[:, 0:1],
                        y1T[:, h * 1024 + t * 512: h * 1024 + t * 512 + 512],
                        start=(h == 0), stop=(h == DCH - 1)),
                        inc="sPE" if h == DCH - 1 else None,
                        drain=(h == DCH - 1))
                P.wait("tensor", "sEV", sq_ev[h])
                pe_here = None
                for t in range(2):
                    pe_here = P.op("tensor", lambda e, h=h, t=t: e.matmul(
                        st_ps[0:1, 2 + t, :], onesc[:, 0:1],
                        y1sq[:, h % 3, t * 512:(t + 1) * 512],
                        start=(h == 0), stop=(h == DCH - 1)), inc="sPE",
                        drain=(h == DCH - 1))
                stat_pe[h] = pe_here

            for h in range(DCH):
                hb = h // 4
                s = hb % 2
                P.wait("tensor", f"sW{wo_loaded[hb][0]}", wo_loaded[hb][1])
                for t in range(2):
                    bank = (2 * h + t) % 4
                    P.wait("tensor", "sDV", opb_ev.get(bank))
                    for m in range(8):
                        P.op("tensor", lambda e, h=h, s=s, m=m, t=t, b=bank: e.matmul(
                            op_ps[:, b, :],
                            wo_sb[:, s, m * 512 + (h % 4) * 128:
                                  m * 512 + (h % 4) * 128 + 128],
                            attnTf[:, m * 1024 + t * 512: m * 1024 + t * 512 + 512],
                            start=(m == 0), stop=False))
                    pe_here = P.op("tensor", lambda e, h=h, b=bank: e.matmul(
                        op_ps[:, b, :], bo_rowb[0:1, h * 128:(h + 1) * 128],
                        onesr512[0:1, :], start=False, stop=True),
                        inc="sPE", drain=True)
                    P.wait("vector", "sPE", pe_here)
                    P.wait("vector", f"sXF{xf_loaded[h][0]}", xf_loaded[h][1])
                    opb_ev[bank] = P.op(
                        "vector", lambda e, h=h, t=t, b=bank, s=h % 2:
                        e.tensor_add(
                            y1T[:, h * 1024 + t * 512: h * 1024 + t * 512 + 512],
                            op_ps[:, b, :],
                            xf32[:, s, t * 512:(t + 1) * 512]), inc="sDV")
                y1_dv[h] = P.cnt["sDV"]
                if h + 2 < DCH:
                    load_xf(h + 2)
                P.wait("scalar", "sDV", y1_dv[h])
                P.wait("scalar", "sPE", stat_pe.get(h - 3))
                sq_ev[h] = P.op("scalar", lambda e, h=h: e.activation(
                    y1sq[:, h % 3, :], y1T[:, h * 1024:(h + 1) * 1024], AF.Square),
                    inc="sEV")
                if h - 2 >= 0:
                    stats_for(h - 2)
                if h % 4 == 3 and hb + 2 <= 7:
                    P.wait("gpsimd", "sPE", P.cnt["sPE"])
                    load_wo(hb + 2)
            stats_for(DCH - 2)
            stats_for(DCH - 1)
            stats_pe_done = P.cnt["sPE"]

            tap_y1_st = None
            if "y1T" in tap_ap:
                P.wait("gpsimd", "sDV", P.cnt["sDV"])
                tap_y1_st = P.op("gpsimd", lambda e: e.dma_start(
                    out=tap_ap["y1T"].rearrange("(c p) n -> p c n", p=128),
                    in_=y1T.rearrange("p (c n) -> p c n", c=32)), inc="sST", k=16)

            P.wait("vector", "sPE", stats_pe_done)
            P.op("vector", lambda e: e.tensor_copy(s1row[:, 0:512], st_ps[0:1, 0, :]))
            P.op("vector", lambda e: e.tensor_copy(s1row[:, 512:1024], st_ps[0:1, 1, :]))
            P.op("vector", lambda e: e.tensor_copy(s2row[:, 0:512], st_ps[0:1, 2, :]))
            dv_here = P.op("vector", lambda e: e.tensor_copy(
                s2row[:, 512:1024], st_ps[0:1, 3, :]), inc="sDV")
            P.wait("vector", "sDV", dv_here)
            P.op("vector", lambda e: e.tensor_scalar_mul(
                s1row[:, :], s1row[:, :], 1.0 / H), inc="sDV")
            P.wait("vector", "sDV", P.cnt["sDV"])
            P.op("vector", lambda e: e.tensor_mul(
                t1row[:, :], s1row[:, :], s1row[:, :]), inc="sDV")
            P.op("vector", lambda e: e.tensor_scalar_mul(
                s2row[:, :], s2row[:, :], 1.0 / H), inc="sDV")
            P.wait("vector", "sDV", P.cnt["sDV"])
            P.op("vector", lambda e: e.tensor_sub(
                s2row[:, :], s2row[:, :], t1row[:, :]), inc="sDV")
            P.wait("vector", "sDV", P.cnt["sDV"])
            dv_here = P.op("vector", lambda e: e.tensor_scalar_add(
                s2row[:, :], s2row[:, :], EPS), inc="sDV")
            P.wait("scalar", "sDV", dv_here)
            ev_here = P.op("scalar", lambda e: e.activation(
                t1row[:, :], s2row[:, :], AF.Sqrt), inc="sEV")
            P.wait("vector", "sEV", ev_here)
            rows_done = P.op("vector", lambda e: e.reciprocal(
                t1row[:, :], t1row[:, :]), inc="sDV")

            P.wait("tensor", "sDV", rows_done)
            for b in range(4):
                P.wait("tensor", "sDV", opb_ev.get(b))
            pe_here = None
            for t in range(2):
                P.op("tensor", lambda e, t=t: e.matmul(
                    op_ps[:, t, :], onesrf[0:1, :], s1row[0:1, t * 512:(t + 1) * 512],
                    start=True, stop=True))
                pe_here = P.op("tensor", lambda e, t=t: e.matmul(
                    op_ps[:, 2 + t, :], onesrf[0:1, :],
                    t1row[0:1, t * 512:(t + 1) * 512],
                    start=True, stop=True), inc="sPE", drain=(t == 1))
            P.wait("vector", "sPE", pe_here)
            for t in range(2):
                P.op("vector", lambda e, t=t: e.tensor_copy(
                    m1e[:, t * 512:(t + 1) * 512], op_ps[:, t, :]))
                P.op("vector", lambda e, t=t: e.tensor_copy(
                    r1e[:, t * 512:(t + 1) * 512], op_ps[:, 2 + t, :]), inc="sDV")
            bcast_done = P.cnt["sDV"]

            P.wait("vector", "sDV", bcast_done)
            P.wait("gpsimd", "sDV", bcast_done)
            if tap_y1_st is not None:
                P.wait("vector", "sST", tap_y1_st)
                P.wait("gpsimd", "sST", tap_y1_st)
            for h in range(DCH):
                if h % 2 == 0:
                    dv_here = P.op("vector", lambda e, h=h: e.tensor_sub(
                        tmpc[:, :], y1T[:, h * 1024:(h + 1) * 1024], m1e[:, :]),
                        inc="sDV")
                    P.wait("vector", "sDV", dv_here)
                    dv_here = P.op("vector", lambda e, h=h: e.tensor_mul(
                        y1T[:, h * 1024:(h + 1) * 1024], tmpc[:, :], r1e[:, :]),
                        inc="sDV")
                    P.wait("vector", "sDV", dv_here)
                else:
                    pl_here = P.op("gpsimd", lambda e, h=h: e.tensor_sub(
                        tmpc2[:, :], y1T[:, h * 1024:(h + 1) * 1024], m1e[:, :]),
                        inc="sPL")
                    P.wait("gpsimd", "sPL", pl_here)
                    pl_here = P.op("gpsimd", lambda e, h=h: e.tensor_mul(
                        y1T[:, h * 1024:(h + 1) * 1024], tmpc2[:, :], r1e[:, :]),
                        inc="sPL")
                    P.wait("gpsimd", "sPL", pl_here)
            ln1_done = P.cnt["sDV"]

            if "ln1T" in tap_ap:
                P.wait("gpsimd", "sDV", ln1_done)
                P.wait("gpsimd", "sPL", P.cnt["sPL"])
                P.op("gpsimd", lambda e: e.dma_start(
                    out=tap_ap["ln1T"].rearrange("(c p) n -> p c n", p=128),
                    in_=y1T.rearrange("p (c n) -> p c n", c=32)), inc="sST", k=16)

            if upto == "C":
                P.wait("gpsimd", "sST", P.cnt["sST"])
            P.barrier()

        sC.close()
        if upto == "C":
            return nc
        ln1T = y1T

        y2 = SL1[:, :]
        sD = ExitStack()
        wstr = sD.enter_context(nc.sbuf_tensor("wstr", [128, 2, 4096], BF16))
        bbb = sD.enter_context(nc.sbuf_tensor("bbb2", [1, H], BF16))
        BTb = sD.enter_context(nc.sbuf_tensor("BTb", [128, 4096], BF16))
        hwT = sD.enter_context(nc.sbuf_tensor("hwT", [128, 1024], BF16))
        re_sb = sD.enter_context(nc.sbuf_tensor("re_sb", [128, 1024], F32))
        S1c = sD.enter_context(nc.sbuf_tensor("S1c", [128, 64], F32))
        S2c = sD.enter_context(nc.sbuf_tensor("S2c", [128, 64], F32))
        sqscr = sD.enter_context(nc.sbuf_tensor("sqscr", [128, 512], F32))
        lnt = sD.enter_context(nc.sbuf_tensor("lnt", [128, 6], F32))
        ffn_ps = sD.enter_context(nc.psum_tensor("ffn_ps", [128, 8, 512], F32))
        if True:

            wstr_loaded = {}
            half_pe = {}

            def load_wstr(idx):
                ot, qq = idx // 4, idx % 4
                s = idx % 2
                P.wait("gpsimd", "sPE", half_pe.get(idx - 2))
                P.op("gpsimd", lambda e, ot=ot, qq=qq, s=s: e.dma_start(
                    out=wstr[:, s, :].rearrange("p (c f) -> p c f", c=8),
                    in_=d_WbT[qq * 1024:(qq + 1) * 1024, ot * 512:(ot + 1) * 512]
                    .rearrange("(c p) f -> p c f", p=128)), inc=f"sW{s}", k=16)
                wstr_loaded[idx] = (f"sW{s}", P.cnt[f"sW{s}"])

            load_wstr(0)
            load_wstr(1)

            sG = ExitStack()
            A2b = sG.enter_context(nc.sbuf_tensor("A2b", [128, 4096], BF16))
            lgt = sG.enter_context(nc.sbuf_tensor("lgt", [128, 64], F32))
            rte = sG.enter_context(nc.sbuf_tensor("rte", [128, 64], F32))
            rteT = sG.enter_context(nc.sbuf_tensor("rteT", [8, 1024], F32))
            gm1 = sG.enter_context(nc.sbuf_tensor("gm1", [128, 1], F32))
            gm2 = sG.enter_context(nc.sbuf_tensor("gm2", [128, 1], F32))
            gmsk = sG.enter_context(nc.sbuf_tensor("gmsk", [128, 8], F32))
            gtmp = sG.enter_context(nc.sbuf_tensor("gtmp", [128, 8], F32))
            gsum = sG.enter_context(nc.sbuf_tensor("gsum", [128, 1], F32))
            if True:

                P.op("gpsimd", lambda e: e.dma_start(out=bbb[:, :], in_=d_bb),
                     inc="sW", k=16)
                bbb_w = P.cnt["sW"]
                P.op("gpsimd", lambda e: e.dma_start(
                    out=A2b.rearrange("p (c f) -> p c f", c=32),
                    in_=d_A2.rearrange("(c p) f -> p c f", p=128)), inc="sW", k=16)
                lora_w = P.op("gpsimd", lambda e: e.dma_start(
                    out=BTb[:, :], in_=d_BT), inc="sW", k=16)

                gate_bank = {}
                for ncg in range(8):
                    pb = ncg % 2
                    P.wait("tensor", "sDV", gate_bank.get(pb))
                    pe_here = None
                    for d in range(DCH):
                        pe_here = P.op("tensor", lambda e, d=d, ncg=ncg, pb=pb:
                                       e.matmul(
                                           ffn_ps[:, pb, 0:8],
                                           ln1T[:, d * 1024 + ncg * 128:
                                                d * 1024 + ncg * 128 + 128],
                                           WgTb[:, d * 8:(d + 1) * 8],
                                           start=(d == 0), stop=(d == DCH - 1)),
                                       inc="sPE" if d == DCH - 1 else None,
                                       drain=(d == DCH - 1))
                    P.wait("vector", "sPE", pe_here)
                    gate_bank[pb] = P.op(
                        "vector", lambda e, ncg=ncg, pb=pb: e.tensor_copy(
                            lgt[:, ncg * 8:(ncg + 1) * 8], ffn_ps[:, pb, 0:8]),
                        inc="sDV")
                if upto == "G1":
                    if "route" in tap_ap:
                        P.wait("gpsimd", "sDV", P.cnt["sDV"])
                        P.op("gpsimd", lambda e: e.dma_start(
                            out=tap_ap["route"].rearrange("(c p) f -> p c f", p=128),
                            in_=lgt.rearrange("p (c f) -> p c f", c=8)),
                            inc="sST", k=16)
                    P.wait("gpsimd", "sST", P.cnt["sST"])
                    P.barrier()
                    sG.close()
                    sD.close()
                    return nc

                P.wait("tensor", wstr_loaded[0][0], wstr_loaded[0][1])
                for ncg in range(4, 8):
                    for d in range(8):
                        P.op("tensor", lambda e, d=d, ncg=ncg: e.matmul(
                            ffn_ps[:, ncg, :],
                            ln1T[:, d * 1024 + ncg * 128:
                                 d * 1024 + ncg * 128 + 128],
                            wstr[:, 0, d * 512:(d + 1) * 512],
                            start=(d == 0), stop=False))

                for ncg in range(8):
                    P.wait("vector", "sDV", P.cnt["sDV"])
                    dv = P.op("vector", lambda e, ncg=ncg: e.reduce_max(
                        gm1[:, :], lgt[:, ncg * 8:(ncg + 1) * 8], axis=AX.X,
                        negate=True), inc="sDV")
                    P.wait("vector", "sDV", dv)
                    dv = P.op("vector", lambda e, ncg=ncg: e.tensor_scalar(
                        gmsk[:, :], lgt[:, ncg * 8:(ncg + 1) * 8],
                        gm1[:, 0:1], -1.0, ALU.add, ALU.mult), inc="sDV")
                    P.wait("vector", "sDV", dv)
                    dv = P.op("vector", lambda e: e.tensor_single_scalar(
                        gmsk[:, :], gmsk[:, :], 0.0, ALU.is_le), inc="sDV")
                    P.wait("vector", "sDV", dv)
                    dv = P.op("vector", lambda e, ncg=ncg: e.scalar_tensor_tensor(
                        gtmp[:, :], gmsk[:, :], -1e30,
                        lgt[:, ncg * 8:(ncg + 1) * 8], ALU.mult, ALU.add), inc="sDV")
                    P.wait("vector", "sDV", dv)
                    dv = P.op("vector", lambda e: e.reduce_max(
                        gm2[:, :], gtmp[:, :], axis=AX.X), inc="sDV")
                    P.wait("vector", "sDV", dv)
                    dv = P.op("vector", lambda e, ncg=ncg: e.tensor_scalar(
                        gmsk[:, :], lgt[:, ncg * 8:(ncg + 1) * 8],
                        gm2[:, 0:1], None, ALU.is_ge), inc="sDV")
                    P.wait("scalar", "sDV", dv)
                    ev = P.op("scalar", lambda e, ncg=ncg: e.activation(
                        gtmp[:, :], lgt[:, ncg * 8:(ncg + 1) * 8], AF.Exp,
                        bias=gm1[:, 0:1]), inc="sEV")
                    P.wait("vector", "sEV", ev)
                    dv = P.op("vector", lambda e: e.tensor_mul(
                        gtmp[:, :], gtmp[:, :], gmsk[:, :]), inc="sDV")
                    P.wait("vector", "sDV", dv)
                    dv = P.op("vector", lambda e: e.reduce_sum(
                        gsum[:, :], gtmp[:, :], axis=AX.X), inc="sDV")
                    P.wait("vector", "sDV", dv)
                    dv = P.op("vector", lambda e: e.reciprocal(
                        gsum[:, :], gsum[:, :]), inc="sDV")
                    P.wait("vector", "sDV", dv)
                    P.op("vector", lambda e, ncg=ncg: e.tensor_scalar_mul(
                        rte[:, ncg * 8:(ncg + 1) * 8], gtmp[:, :], gsum[:, 0:1]),
                        inc="sDV")
                route_done = P.cnt["sDV"]

                if "route" in tap_ap:
                    P.wait("gpsimd", "sDV", route_done)
                    P.op("gpsimd", lambda e: e.dma_start(
                        out=tap_ap["route"].rearrange("(c p) f -> p c f", p=128),
                        in_=rte.rearrange("p (c f) -> p c f", c=8)), inc="sST", k=16)

                if upto == "G2":
                    P.wait("gpsimd", "sST", P.cnt["sST"])
                    P.barrier()
                    sG.close()
                    sD.close()
                    return nc

                P.wait("tensor", "sDV", route_done)
                for ncg in range(8):
                    pb = ncg % 2
                    P.wait("tensor", "sDV", gate_bank.get(pb))
                    pe_here = P.op("tensor", lambda e, ncg=ncg, pb=pb: e.transpose(
                        ffn_ps[0:8, pb, 0:128],
                        rte[:, ncg * 8:(ncg + 1) * 8], identf[:, :]),
                        inc="sPE", drain=True)
                    P.wait("vector", "sPE", pe_here)
                    gate_bank[pb] = P.op(
                        "vector", lambda e, ncg=ncg, pb=pb: e.tensor_copy(
                            rteT[0:8, ncg * 128:(ncg + 1) * 128],
                            ffn_ps[0:8, pb, 0:128]), inc="sDV")
                rteT_done = P.cnt["sDV"]

                if upto == "G3":
                    P.wait("gpsimd", "sST", P.cnt["sST"])
                    P.barrier()
                    sG.close()
                    sD.close()
                    return nc

                P.wait("tensor", "sDV", rteT_done)
                re_done = None
                re_evt = {}
                for t in range(2):
                    pe_here = P.op("tensor", lambda e, t=t: e.matmul(
                        ffn_ps[:, 2 + t, :], EXP8b[:, :],
                        rteT[0:8, t * 512:(t + 1) * 512],
                        start=True, stop=True), inc="sPE", drain=True)
                    P.wait("vector", "sPE", pe_here)
                    re_done = P.op("vector", lambda e, t=t: e.tensor_copy(
                        re_sb[:, t * 512:(t + 1) * 512], ffn_ps[:, 2 + t, :]),
                        inc="sDV")
                    re_evt[t] = re_done
                if upto == "G4":
                    P.wait("gpsimd", "sDV", P.cnt["sDV"])
                    P.wait("gpsimd", "sST", P.cnt["sST"])
                    P.barrier()
                    sG.close()
                    sD.close()
                    return nc

                P.wait("tensor", "sW", lora_w)
                for t in range(2):
                    P.wait("tensor", "sDV", re_evt[t])
                    pe_here = None
                    for d in range(DCH):
                        pe_here = P.op("tensor", lambda e, d=d, t=t: e.matmul(
                            ffn_ps[:, 2 + t, :], A2b[:, d * 128:(d + 1) * 128],
                            ln1T[:, d * 1024 + t * 512: d * 1024 + t * 512 + 512],
                            start=(d == 0), stop=(d == DCH - 1)),
                            inc="sPE" if d == DCH - 1 else None,
                            drain=(d == DCH - 1))
                    P.wait("vector", "sPE", pe_here)
                    P.wait("vector", "sDV", re_done)
                    P.op("vector", lambda e, t=t: e.tensor_mul(
                        hwT[:, t * 512:(t + 1) * 512], ffn_ps[:, 2 + t, :],
                        re_sb[:, t * 512:(t + 1) * 512]), inc="sDV")
                hw_done = P.cnt["sDV"]
                if "hwT" in tap_ap:
                    P.wait("gpsimd", "sDV", hw_done)
                    P.op("gpsimd", lambda e: e.dma_start(
                        out=tap_ap["hwT"].rearrange("(c p) n -> p c n", p=128),
                        in_=hwT.rearrange("p (c n) -> p c n", c=1)), inc="sST", k=16)

            sG.close()

            if upto == "G":
                P.wait("gpsimd", "sST", P.cnt["sST"])
                P.barrier()
                sD.close()
                return nc

            evac_dv = {}
            sq_last = None
            for ot in range(8):
                for qq in range(4):
                    idx = ot * 4 + qq
                    slot = idx % 2
                    P.wait("tensor", wstr_loaded[idx][0], wstr_loaded[idx][1])
                    if ot == 0 and qq == 0:
                        P.wait("tensor", "sDV", rteT_done)
                        P.wait("tensor", "sDV", hw_done)
                    if qq == 0 and ot > 0:
                        for ncg in range(8):
                            P.wait("tensor", "sDV", evac_dv.get(ncg))
                    for ncg in range(8):
                        if ot == 0 and qq == 0 and ncg >= 4:
                            continue
                        for d in range(8):
                            dg = qq * 8 + d
                            last = (d == 7 and
                                    ncg == (3 if (ot == 0 and qq == 0) else 7))
                            half_pe[idx] = P.op(
                                "tensor",
                                lambda e, s=slot, d=d, dg=dg, ncg=ncg: e.matmul(
                                    ffn_ps[:, ncg, :],
                                    ln1T[:, dg * 1024 + ncg * 128:
                                         dg * 1024 + ncg * 128 + 128],
                                    wstr[:, s, d * 512:(d + 1) * 512],
                                    start=(dg == 0), stop=False),
                                inc="sPE" if last else None)
                    if idx + 2 < 32:
                        load_wstr(idx + 2)
                for ncg in range(8):
                    P.op("tensor", lambda e, ncg=ncg, ot=ot: e.matmul(
                        ffn_ps[:, ncg, :], hwT[:, ncg * 128:(ncg + 1) * 128],
                        BTb[:, ot * 512:(ot + 1) * 512], start=False, stop=False))
                    if ot == 0 and ncg == 0:
                        P.wait("tensor", "sW", bbb_w)
                    P.op("tensor", lambda e, ncg=ncg, ot=ot: e.matmul(
                        ffn_ps[:, ncg, :], onesr[0:1, :],
                        bbb[0:1, ot * 512:(ot + 1) * 512], start=False, stop=False))
                    pe_here = None
                    for j in range(4):
                        pe_here = P.op("tensor", lambda e, ncg=ncg, ot=ot, j=j: e.matmul(
                            ffn_ps[:, ncg, j * 128:(j + 1) * 128],
                            ln1T[:, (ot * 4 + j) * 1024 + ncg * 128:
                                 (ot * 4 + j) * 1024 + ncg * 128 + 128],
                            identb[:, :], start=False, stop=(j == 3)),
                            inc="sPE" if j == 3 else None, drain=(j == 3))
                    P.wait("scalar", "sPE", pe_here)
                    if ot == 7 and ncg == 7:
                        P.delay("scalar", dly)
                    sq_last = P.op("scalar", lambda e, ncg=ncg, ot=ot: e.activation(
                        sqscr[:, :], ffn_ps[:, ncg, :], AF.Square,
                        accum_out=S2c[:, ncg * 8 + ot: ncg * 8 + ot + 1]), inc="sEV")
                    P.wait("vector", "sEV", sq_last)
                    evac_dv[ncg] = P.op(
                        "vector", lambda e, ncg=ncg, ot=ot: e.tensor_scalar(
                            y2[:, ncg * 4096 + ot * 512: ncg * 4096 + ot * 512 + 512],
                            ffn_ps[:, ncg, :], 1.0, 0.0, ALU.mult, ALU.add,
                            accum_out=S1c[:, ncg * 8 + ot: ncg * 8 + ot + 1]),
                        inc="sDV")
            ffn_done = P.cnt["sDV"]

            if "y2" in tap_ap:
                P.wait("gpsimd", "sDV", ffn_done)
                P.op("gpsimd", lambda e: e.dma_start(
                    out=tap_ap["y2"].rearrange("(c p) f -> p c f", p=128),
                    in_=y2.rearrange("p (c f) -> p c f", c=8)), inc="sST", k=16)

            outst = sD.enter_context(nc.sbuf_tensor("outst", [128, 2, 1024], F32))
            if True:
                st_done = {}
                P.wait("vector", "sDV", ffn_done)
                P.wait("vector", "sEV", sq_last)
                for ncg in range(8):
                    P.op("vector", lambda e, ncg=ncg: e.reduce_sum(
                        lnt[:, 0:1], S1c[:, ncg * 8:(ncg + 1) * 8], axis=AX.X),
                        inc="sDV")
                    dv = P.op("vector", lambda e, ncg=ncg: e.reduce_sum(
                        lnt[:, 1:2], S2c[:, ncg * 8:(ncg + 1) * 8], axis=AX.X),
                        inc="sDV")
                    P.wait("vector", "sDV", dv)
                    P.op("vector", lambda e: e.tensor_scalar_mul(
                        lnt[:, 0:1], lnt[:, 0:1], 1.0 / H), inc="sDV")
                    dv = P.op("vector", lambda e: e.tensor_scalar_mul(
                        lnt[:, 1:2], lnt[:, 1:2], 1.0 / H), inc="sDV")
                    P.wait("vector", "sDV", dv)
                    dv = P.op("vector", lambda e: e.tensor_mul(
                        lnt[:, 2:3], lnt[:, 0:1], lnt[:, 0:1]), inc="sDV")
                    P.wait("vector", "sDV", dv)
                    dv = P.op("vector", lambda e: e.tensor_sub(
                        lnt[:, 3:4], lnt[:, 1:2], lnt[:, 2:3]), inc="sDV")
                    P.wait("vector", "sDV", dv)
                    dv = P.op("vector", lambda e: e.tensor_scalar_add(
                        lnt[:, 3:4], lnt[:, 3:4], EPS), inc="sDV")
                    P.wait("scalar", "sDV", dv)
                    ev = P.op("scalar", lambda e: e.activation(
                        lnt[:, 4:5], lnt[:, 3:4], AF.Sqrt), inc="sEV")
                    P.wait("vector", "sEV", ev)
                    dv = P.op("vector", lambda e: e.reciprocal(
                        lnt[:, 5:6], lnt[:, 4:5]), inc="sDV")
                    P.wait("vector", "sDV", dv)
                    for q4 in range(4):
                        s = q4 % 2
                        P.wait("vector", f"sSO{s}", st_done.get(s))
                        dv = P.op("vector", lambda e, ncg=ncg, q4=q4, s=s:
                                  e.tensor_scalar(
                                      outst[:, s, :],
                                      y2[:, ncg * 4096 + q4 * 1024:
                                         ncg * 4096 + q4 * 1024 + 1024],
                                      lnt[:, 0:1], lnt[:, 5:6],
                                      ALU.subtract, ALU.mult), inc="sDV")
                        P.wait("sync", "sDV", dv)
                        st_done[s] = P.op(
                            "sync", lambda e, ncg=ncg, q4=q4, s=s: e.dma_start(
                                out=d_out[ncg * 128:(ncg + 1) * 128,
                                          q4 * 1024:(q4 + 1) * 1024],
                                in_=outst[:, s, :]), inc=f"sSO{s}", k=16)

                P.wait("gpsimd", "sSO0", P.cnt["sSO0"])
                P.wait("gpsimd", "sSO1", P.cnt["sSO1"])
                P.wait("gpsimd", "sST", P.cnt["sST"])
                P.barrier()
        sD.close()

    return nc


def _prep_inputs(L0_attn_x, L0_attn_Wqkv, L0_attn_bqkv, L0_attn_Wo, L0_attn_bo,
                 L0_ffn_W_base, L0_ffn_b_base, L0_ffn_W_gate, L0_ffn_A, L0_ffn_B):
    bf = ml_dtypes.bfloat16
    f32 = np.float32
    com = {
        "Wqkv": np.ascontiguousarray(np.asarray(L0_attn_Wqkv, dtype=f32)),
        "bqkv_pc": np.ascontiguousarray(np.asarray(L0_attn_bqkv, dtype=f32)
                                        .reshape(24, 128).T),
        "bv_row": np.ascontiguousarray(np.asarray(L0_attn_bqkv, dtype=f32)
                                       [2048:3072].reshape(1, DH)),
        "Wo": np.ascontiguousarray(np.asarray(L0_attn_Wo, dtype=f32)),
        "bo_pc": np.ascontiguousarray(np.asarray(L0_attn_bo, dtype=f32)
                                      .reshape(32, 128).T),
        "WbT": np.ascontiguousarray(np.asarray(L0_ffn_W_base, dtype=f32).T),
        "bb_row": np.ascontiguousarray(np.asarray(L0_ffn_b_base, dtype=f32)
                                       .reshape(1, H)),
        "WgT": np.ascontiguousarray(np.asarray(L0_ffn_W_gate, dtype=f32).T),
        "A2": np.ascontiguousarray(np.asarray(L0_ffn_A, dtype=f32)
                                   .transpose(2, 0, 1).reshape(H, 128)),
        "BT": np.ascontiguousarray(np.asarray(L0_ffn_B, dtype=f32)
                                   .transpose(0, 2, 1).reshape(128, H)),
        "EXP8": np.kron(np.eye(8, dtype=f32), np.ones((1, 16), dtype=f32)),
        "identb": np.eye(128, dtype=bf),
        "identf": np.eye(128, dtype=f32),
        "onescol": np.ones((128, 1), dtype=bf),
        "onesrow": np.ones((1, 128), dtype=bf),
        "onesrow512": np.ones((1, 512), dtype=bf),
        "bo_row": np.ascontiguousarray(np.asarray(L0_attn_bo, dtype=f32)
                                       .reshape(1, H)),
        "onesrowf": np.ones((1, 128), dtype=f32),
    }
    x = np.asarray(L0_attn_x, dtype=f32)
    in_maps = []
    for c in range(8):
        b, q0 = c // 4, (c % 4) * NT
        m = dict(com)
        m["xT"] = np.ascontiguousarray(x[b].T[:, q0:q0 + NT])
        g, my = c // 4, c % 4
        off = [0] * 4
        for j in (0, 1, 2, 3):
            rank = 4 * g + (my + j) % 4
            off[j] = rank * 2 * DH * NT
        m["kvoff"] = np.asarray([off], dtype=np.int32)
        in_maps.append(m)
    return in_maps


def kernel(**inputs):
    in_maps = _prep_inputs(**inputs)
    nc = build_nc(taps=DEBUG_TAPS, upto=UPTO)
    res = run_bass_kernel_spmd(nc, in_maps, core_ids=list(range(8)))
    out = np.zeros((2, S, H), dtype=np.float32)
    if UPTO == "D":
        for c in range(8):
            b, q0 = c // 4, (c % 4) * NT
            out[b, q0:q0 + NT, :] = np.asarray(res.results[c]["out"])
    if DEBUG_TAPS:
        kernel.last_results = res
    return out


# version 08
# speedup vs baseline: 1.0554x; 1.0128x over previous
from contextlib import ExitStack

import numpy as np
import ml_dtypes

import concourse.bass as bass
import concourse.mybir as mybir
from concourse.bass_utils import run_bass_kernel_spmd

F32 = mybir.dt.float32
BF16 = mybir.dt.bfloat16
AF = mybir.ActivationFunctionType
ALU = mybir.AluOpType
AX = mybir.AxisListType

H = 4096
DH = 1024
NT = 1024
S = 4096
NCH = NT // 128
DCH = H // 128
EPS = 1e-5
GROUPS = [[0, 1, 2, 3], [4, 5, 6, 7]]

DEBUG_TAPS = ()
UPTO = "D"

ENGS = ("tensor", "vector", "scalar", "gpsimd", "sync")


class Prog:

    def __init__(self, nc, stack):
        self.nc = nc
        self.stack = stack
        self.streams = {e: [] for e in ENGS}
        self.cnt = {}
        self.hdl = {}
        self.nblk = 0

    def sem(self, name):
        self.hdl[name] = self.stack.enter_context(self.nc.semaphore(name))
        self.cnt[name] = 0

    def op(self, eng, builder, inc=None, k=1, drain=False):
        if inc is not None and drain:
            assert eng == "tensor"
            self.cnt[inc] += k
            h = self.hdl[inc]

            def emit(e, b=builder, h=h, k=k):
                b(e)
                e.drain().then_inc(h, k)
            self.streams[eng].append(emit)
            return self.cnt[inc]
        if inc is None:
            self.streams[eng].append(lambda e, b=builder: b(e))
            return None
        self.cnt[inc] += k
        h = self.hdl[inc]
        self.streams[eng].append(lambda e, b=builder, h=h, k=k: b(e).then_inc(h, k))
        return self.cnt[inc]

    def delay(self, eng, scratch):
        if eng == "vector":
            self.streams[eng].append(
                lambda e, s=scratch: e.tensor_copy(s[0:1, 0:1], s[0:1, 1:2]))
        else:
            self.streams[eng].append(
                lambda e, s=scratch: e.copy(s[0:1, 0:1], s[0:1, 1:2]))

    def wait(self, eng, sname, val):
        if val is None or val <= 0:
            return
        h = self.hdl[sname]
        self.streams[eng].append(lambda e, h=h, v=val: e.wait_ge(h, v))

    def barrier(self):
        self.nblk += 1
        streams, self.streams = self.streams, {e: [] for e in ENGS}
        with self.nc.Block(f"ph{self.nblk}") as blk:
            for eng in ENGS:
                fns = streams[eng]
                if not fns:
                    continue
                getattr(blk, eng)(lambda e, fns=fns: [f(e) for f in fns])


def build_nc(taps=(), upto="D"):
    nc = bass.Bass("TRN2", target_bir_lowering=False)

    d_xT = nc.dram_tensor("xT", [H, NT], F32, kind="ExternalInput").ap()
    d_Wqkv = nc.dram_tensor("Wqkv", [H, 3 * DH], F32, kind="ExternalInput").ap()
    d_bqkv = nc.dram_tensor("bqkv_pc", [128, 24], F32, kind="ExternalInput").ap()
    d_bv = nc.dram_tensor("bv_row", [1, DH], F32, kind="ExternalInput").ap()
    d_Wo = nc.dram_tensor("Wo", [DH, H], F32, kind="ExternalInput").ap()
    d_bo = nc.dram_tensor("bo_pc", [128, 32], F32, kind="ExternalInput").ap()
    d_WbT = nc.dram_tensor("WbT", [H, H], F32, kind="ExternalInput").ap()
    d_bb = nc.dram_tensor("bb_row", [1, H], F32, kind="ExternalInput").ap()
    d_WgT = nc.dram_tensor("WgT", [H, 8], F32, kind="ExternalInput").ap()
    d_A2 = nc.dram_tensor("A2", [H, 128], F32, kind="ExternalInput").ap()
    d_BT = nc.dram_tensor("BT", [128, H], F32, kind="ExternalInput").ap()
    d_EXP8 = nc.dram_tensor("EXP8", [8, 128], F32, kind="ExternalInput").ap()
    d_idb = nc.dram_tensor("identb", [128, 128], BF16, kind="ExternalInput").ap()
    d_idf = nc.dram_tensor("identf", [128, 128], F32, kind="ExternalInput").ap()
    d_onc = nc.dram_tensor("onescol", [128, 1], BF16, kind="ExternalInput").ap()
    d_onr = nc.dram_tensor("onesrow", [1, 128], BF16, kind="ExternalInput").ap()
    d_onr512 = nc.dram_tensor("onesrow512", [1, 512], BF16, kind="ExternalInput").ap()
    d_bor = nc.dram_tensor("bo_row", [1, H], F32, kind="ExternalInput").ap()
    d_onrf = nc.dram_tensor("onesrowf", [1, 128], F32, kind="ExternalInput").ap()
    d_out = nc.dram_tensor("out", [NT, H], F32, kind="ExternalOutput").ap()

    cc_in = nc.dram_tensor("cc_in", [2 * DH, NT], BF16, kind="Internal")
    cc_out = nc.dram_tensor("cc_out", [8 * 2 * DH, NT], BF16, kind="Internal",
                            addr_space="Shared")
    d_kvoff = nc.dram_tensor("kvoff", [1, 4], mybir.dt.int32,
                             kind="ExternalInput").ap()

    tap_shapes = {
        "Kown": [DH, NT], "Vown": [NT, DH], "Qb": [DH, NT],
        "attn": [NT, DH], "y1T": [H, NT], "ln1T": [H, NT],
        "route": [NT, 8], "hwT": [128, NT], "y2": [NT, H],
    }
    tap_ap = {t: nc.dram_tensor("tap_" + t, tap_shapes[t], F32,
                                kind="ExternalOutput").ap() for t in taps}

    with ExitStack() as stack:
        ent = stack.enter_context
        P = Prog(nc, stack)
        for s in ("sX", "sW", "sSY", "sPE", "sEV", "sDV", "sCC", "sST", "sPL",
                  "sW0", "sW1", "sWq0", "sWq1", "sXF0", "sXF1", "sKV0", "sKV1",
                  "sSO0", "sSO1"):
            P.sem(s)

        SL1 = ent(nc.sbuf_tensor("SL1", [128, 32768], BF16))
        SL2 = ent(nc.sbuf_tensor("SL2", [128, 32768], BF16))
        bq_pc = ent(nc.sbuf_tensor("bq_pc", [128, 24], F32))
        bo_pc = ent(nc.sbuf_tensor("sb_bo_pc", [128, 32], F32))
        identb = ent(nc.sbuf_tensor("sb_identb", [128, 128], BF16))
        identf = ent(nc.sbuf_tensor("sb_identf", [128, 128], F32))
        onesc = ent(nc.sbuf_tensor("onesc", [128, 1], BF16))
        onesr = ent(nc.sbuf_tensor("onesr", [1, 128], BF16))
        onesr512 = ent(nc.sbuf_tensor("onesr512", [1, 512], BF16))
        onesrf = ent(nc.sbuf_tensor("onesrf", [1, 128], F32))
        WgTb = ent(nc.sbuf_tensor("WgTb", [128, 32 * 8], BF16))
        EXP8b = ent(nc.sbuf_tensor("EXP8b", [8, 128], F32))
        dly = ent(nc.sbuf_tensor("dly", [1, 8], F32))
        kvoffb = ent(nc.sbuf_tensor("kvoffb", [1, 4], mybir.dt.int32))
        stackAB = ExitStack()
        Qb = stackAB.enter_context(nc.sbuf_tensor("Qb", [128, 8192], BF16))
        Kown = stackAB.enter_context(nc.sbuf_tensor("Kown", [128, 8192], BF16))
        rk = stackAB.enter_context(nc.gpsimd.register("rk"))
        rv = stackAB.enter_context(nc.gpsimd.register("rv"))

        xTb = SL1[:, :]
        wkq = [SL2[:, i * 4096:(i + 1) * 4096] for i in range(4)]
        wv = [SL2[:, 16384 + i * 8192: 16384 + (i + 1) * 8192] for i in range(2)]

        sA = ExitStack()
        arena = sA.enter_context(nc.psum_tensor("arena", [128, 8, 512], F32))
        Vown = sA.enter_context(nc.sbuf_tensor("Vown", [128, 8192], BF16))
        bvb = sA.enter_context(nc.sbuf_tensor("bvb", [1, DH], BF16))
        if True:
            P.op("gpsimd", lambda e: e.dma_start(
                out=xTb.rearrange("p (c n) -> p c n", c=32),
                in_=d_xT.rearrange("(c p) n -> p c n", p=128)), inc="sX", k=16)
            P.op("gpsimd", lambda e: e.dma_start(out=bvb[:, :], in_=d_bv),
                 inc="sX", k=16)
            P.op("gpsimd", lambda e: e.dma_start(
                out=WgTb.rearrange("p (c f) -> p c f", c=32),
                in_=d_WgT.rearrange("(c p) f -> p c f", p=128)), inc="sX", k=16)
            x_done = bv_done = P.cnt["sX"]

            for dst, src in ((bq_pc, d_bqkv), (bo_pc, d_bo), (identb, d_idb),
                             (identf, d_idf), (onesc, d_onc), (onesr, d_onr),
                             (onesrf, d_onrf), (EXP8b, d_EXP8),
                             (onesr512, d_onr512), (kvoffb, d_kvoff)):
                P.op("sync", lambda e, a=dst, b=src: e.dma_start(out=a[:, :], in_=b),
                     inc="sSY", k=16)
            sy_const = P.cnt["sSY"]

            pe_stripe = {}
            stripe_w = {}
            bank_ev = {}

            def load_stripe(seq, ci):
                slot = seq % 4
                prev = pe_stripe.get(seq - 4)
                if prev is not None:
                    P.wait("gpsimd", "sPE", prev)
                sem = f"sW{slot % 2}" if slot < 2 else f"sWq{slot - 2}"
                P.op("gpsimd", lambda e, ci=ci, s=slot: e.dma_start(
                    out=wkq[s].rearrange("p (c f) -> p c f", c=32),
                    in_=d_Wqkv[:, ci * 128:(ci + 1) * 128]
                    .rearrange("(c p) f -> p c f", p=128)), inc=sem, k=16)
                stripe_w[seq] = (sem, P.cnt[sem])

            def proj_ft(kind, idx, ci, dst, scale):
                seq = idx if kind == "K" else 8 + idx
                slot = seq % 4
                if seq not in stripe_w:
                    load_stripe(seq, ci)
                P.wait("tensor", stripe_w[seq][0], stripe_w[seq][1])
                if kind == "K" and idx == 0:
                    P.wait("tensor", "sX", x_done)
                if kind == "Q" and idx == 0:
                    P.wait("tensor", "sEV", kv_evac_total[0])
                for ntile in range(2):
                    bank = (2 * idx + ntile) % 4 + (4 if kind == "Q" else 0)
                    P.wait("tensor", "sEV", bank_ev.get(bank))
                    pe_here = None
                    for d in range(DCH):
                        pe_here = P.op(
                            "tensor", lambda e, s=slot, d=d, nt=ntile, b=bank: e.matmul(
                                arena[:, b, :],
                                wkq[s][:, d * 128:(d + 1) * 128],
                                xTb[:, d * 1024 + nt * 512: d * 1024 + nt * 512 + 512],
                                start=(d == 0), stop=(d == DCH - 1)),
                            inc="sPE" if d == DCH - 1 else None,
                            drain=(d == DCH - 1))
                    P.wait("vector", "sPE", pe_here)
                    if kind == "Q" and idx == 7 and ntile == 1:
                        P.delay("vector", dly)
                    P.wait("vector", "sSY", sy_const)
                    bias_ap = bq_pc[:, ci:ci + 1]
                    bank_ev[bank] = P.op(
                        "vector", lambda e, dst=dst, i=idx, nt=ntile, b=bank,
                        ba=bias_ap, sc=scale: e.tensor_scalar(
                            dst[:, i * 1024 + nt * 512: i * 1024 + nt * 512 + 512],
                            arena[:, b, :], ba, sc, ALU.add, ALU.mult), inc="sEV")
                pe_stripe[seq] = P.cnt["sPE"]

            kv_evac_total = [0]
            for i in range(8):
                proj_ft("K", i, 8 + i, Kown, 1.0)
            k_evac_total = P.cnt["sEV"]

            for dt_ in range(2):
                for hhalf in range(2):
                    vidx = 2 * dt_ + hhalf
                    slot = vidx % 2
                    prev = pe_stripe.get(("V", vidx - 2))
                    if prev is not None:
                        P.wait("gpsimd", "sPE", prev)
                    P.op("gpsimd", lambda e, dt_=dt_, h=hhalf, s=slot: e.dma_start(
                        out=wv[s].rearrange("p (c f) -> p c f", c=16),
                        in_=d_Wqkv[h * 2048:(h + 1) * 2048,
                                   2048 + dt_ * 512: 2048 + (dt_ + 1) * 512]
                        .rearrange("(c p) f -> p c f", p=128)), inc=f"sW{slot}", k=16)
                    P.wait("tensor", f"sW{slot}", P.cnt[f"sW{slot}"])
                    if vidx == 0:
                        P.wait("tensor", "sEV", k_evac_total)
                    if dt_ == 1 and hhalf == 0:
                        P.wait("tensor", "sEV", P.cnt["sEV"])
                    pe_here = None
                    for d in range(16):
                        dg = hhalf * 16 + d
                        for ncg in range(8):
                            last = (d == 15 and ncg == 7)
                            pe_here = P.op(
                                "tensor",
                                lambda e, s=slot, d=d, dg=dg, ncg=ncg: e.matmul(
                                    arena[:, ncg, :],
                                    xTb[:, dg * 1024 + ncg * 128:
                                        dg * 1024 + ncg * 128 + 128],
                                    wv[s][:, d * 512:(d + 1) * 512],
                                    start=(dg == 0), stop=False),
                                inc="sPE" if last else None)
                    pe_stripe[("V", vidx)] = pe_here
                P.wait("tensor", "sX", bv_done)
                for ncg in range(8):
                    pe_here = P.op("tensor", lambda e, ncg=ncg, dt_=dt_: e.matmul(
                        arena[:, ncg, :], onesr[0:1, :],
                        bvb[0:1, dt_ * 512:(dt_ + 1) * 512],
                        start=False, stop=True), inc="sPE", drain=True)
                    P.wait("scalar", "sPE", pe_here)
                    P.op("scalar", lambda e, ncg=ncg, dt_=dt_: e.activation(
                        Vown[:, ncg * 1024 + dt_ * 512: ncg * 1024 + dt_ * 512 + 512],
                        arena[:, ncg, :], AF.Copy), inc="sEV")
            kv_evac_total[0] = P.cnt["sEV"]

            for qi in range(4):
                load_stripe(8 + qi, qi)

            P.wait("gpsimd", "sEV", kv_evac_total[0])
            P.op("gpsimd", lambda e: e.dma_start(
                out=cc_in.ap()[0:DH, :].rearrange("(c p) n -> p c n", p=128),
                in_=Kown.rearrange("p (c n) -> p c n", c=8)), inc="sW", k=16)
            P.op("gpsimd", lambda e: e.dma_start(
                out=cc_in.ap()[DH:2 * DH, :].rearrange("(c p) n -> p c n", p=128),
                in_=Vown.rearrange("p (c n) -> p c n", c=8)), inc="sW", k=16)

            P.wait("gpsimd", "sW", P.cnt["sW"])
            P.op("gpsimd", lambda e: e.collective_compute(
                "AllGather", ALU.bypass, replica_groups=[list(range(8))],
                ins=[cc_in.ap().opt()], outs=[cc_out.ap().opt()]), inc="sCC")

            for i in range(8):
                proj_ft("Q", i, i, Qb, 1.0 / 32.0)

            if "Kown" in tap_ap:
                P.wait("gpsimd", "sEV", kv_evac_total[0])
                P.op("gpsimd", lambda e: e.dma_start(
                    out=tap_ap["Kown"].rearrange("(c p) n -> p c n", p=128),
                    in_=Kown.rearrange("p (c n) -> p c n", c=8)), inc="sST", k=16)
                P.op("gpsimd", lambda e: e.dma_start(
                    out=tap_ap["Vown"].rearrange("(c p) n -> p c n", p=128),
                    in_=Vown.rearrange("p (c n) -> p c n", c=8)), inc="sST", k=16)
            if "Qb" in tap_ap:
                P.wait("gpsimd", "sEV", P.cnt["sEV"])
                P.op("gpsimd", lambda e: e.dma_start(
                    out=tap_ap["Qb"].rearrange("(c p) n -> p c n", p=128),
                    in_=Qb.rearrange("p (c n) -> p c n", c=8)), inc="sST", k=16)

            if upto == "A":
                P.wait("gpsimd", "sST", P.cnt["sST"])
            P.barrier()

        sA.close()
        if upto == "A":
            stackAB.close()
            return nc

        attnTf = SL1[:, 0:8192]
        attn_bf = SL1[:, 8192:16384]
        Kblk = [SL2[:, i * 8192:(i + 1) * 8192] for i in range(2)]
        Vblk = [SL2[:, 16384 + i * 8192: 16384 + (i + 1) * 8192] for i in range(2)]

        sB = ExitStack()
        acc = sB.enter_context(nc.sbuf_tensor("acc", [128, 8192], F32))
        Pb = sB.enter_context(nc.sbuf_tensor("Pb", [128, 2, 1024], BF16))
        PT = sB.enter_context(nc.sbuf_tensor("PT", [128, 2, 1024], BF16))
        mrun = sB.enter_context(nc.sbuf_tensor("mrun", [128, 8], F32))
        srun = sB.enter_context(nc.sbuf_tensor("srun", [128, 8], F32))
        mb2 = sB.enter_context(nc.sbuf_tensor("mb2", [128, 2, 2], F32))
        mnew = sB.enter_context(nc.sbuf_tensor("mnew", [128, 2], F32))
        negm = sB.enter_context(nc.sbuf_tensor("negm", [128, 2], F32))
        alpha = sB.enter_context(nc.sbuf_tensor("alpha", [128, 2], F32))
        sblk = sB.enter_context(nc.sbuf_tensor("sblk", [128, 2, 2], F32))
        rsum = sB.enter_context(nc.sbuf_tensor("rsum", [128, 1], F32))
        att_ps = sB.enter_context(nc.psum_tensor("att_ps", [128, 6, 512], F32))
        pt_ps = sB.enter_context(nc.psum_tensor("pt_ps", [128, 2, 1024], BF16))
        if True:

            kv_loaded = {}
            pe_blk = {}

            kv_pattern = [[1024, 128], [131072, 8], [1, 1024]]

            def load_kv(r):
                s = r % 2
                P.op("gpsimd", lambda e, r=r: e.reg_load(rk, kvoffb[0:1, r:r + 1]))
                P.op("gpsimd", lambda e: e.reg_add(rv, rk, DH * NT))
                if r > 0:
                    P.op("gpsimd", lambda e, s=s: e.dma_start(
                        out=Kblk[s].rearrange("p (c n) -> p c n", c=8),
                        in_=bass.AP(cc_out, rk, kv_pattern)), inc=f"sKV{s}", k=16)
                P.op("gpsimd", lambda e, s=s: e.dma_start(
                    out=Vblk[s].rearrange("p (c n) -> p c n", c=8),
                    in_=bass.AP(cc_out, rv, kv_pattern)), inc=f"sKV{s}", k=16)
                kv_loaded[r] = (s, P.cnt[f"sKV{s}"])

            P.wait("gpsimd", "sSY", sy_const)
            P.wait("gpsimd", "sCC", 1)
            load_kv(0)

            dv_bank = {}
            pt_bank = {}
            pb_free = {}
            pe_sc = {}
            exp_ev = {}

            def emit_scores(r, qc):
                kb = kvbuf(r)[0]
                p = qc % 2
                for t in range(2):
                    bank = 2 * p + t
                    fb = dv_bank.get(bank)
                    if fb is not None:
                        P.wait("tensor", fb[0], fb[1])
                    pe_here = None
                    for m in range(8):
                        pe_here = P.op(
                            "tensor",
                            lambda e, kb=kb, m=m, qc=qc, t=t, b=bank: e.matmul(
                                att_ps[:, b, :],
                                Qb[:, m * 1024 + qc * 128:
                                   m * 1024 + qc * 128 + 128],
                                kb[:, m * 1024 + t * 512:
                                   m * 1024 + t * 512 + 512],
                                start=(m == 0), stop=(m == 7)),
                            inc="sPE" if m == 7 else None, drain=(m == 7))
                    pe_sc[(qc, t)] = pe_here

            def emit_soft(r, qc):
                p = qc % 2
                mx_done = None
                for t in range(2):
                    bank = 2 * p + t
                    P.wait("vector", "sPE", pe_sc[(qc, t)])
                    mx_done = P.op("vector", lambda e, p=p, t=t, b=bank: e.reduce_max(
                        mb2[:, p, t:t + 1], att_ps[:, b, :], axis=AX.X), inc="sDV")
                P.wait("vector", "sDV", mx_done)
                if r == 0:
                    P.op("vector", lambda e, qc=qc, p=p: e.reduce_max(
                        mrun[:, qc:qc + 1], mb2[:, p, :], axis=AX.X), inc="sDV")
                    P.wait("vector", "sDV", P.cnt["sDV"])
                    dv_here = P.op("vector", lambda e, qc=qc, p=p: e.tensor_scalar_mul(
                        negm[:, p:p + 1], mrun[:, qc:qc + 1], -1.0), inc="sDV")
                    P.wait("vector", "sDV", dv_here)
                else:
                    P.op("vector", lambda e, p=p: e.reduce_max(
                        mnew[:, p:p + 1], mb2[:, p, :], axis=AX.X), inc="sDV")
                    P.wait("vector", "sDV", P.cnt["sDV"])
                    P.op("vector", lambda e, qc=qc, p=p: e.tensor_max(
                        mnew[:, p:p + 1], mnew[:, p:p + 1], mrun[:, qc:qc + 1]),
                        inc="sDV")
                    P.wait("vector", "sDV", P.cnt["sDV"])
                    dv_here = P.op("vector", lambda e, p=p: e.tensor_scalar_mul(
                        negm[:, p:p + 1], mnew[:, p:p + 1], -1.0), inc="sDV")
                    P.wait("vector", "sDV", dv_here)
                    P.wait("scalar", "sDV", dv_here)
                    al_done = P.op("scalar", lambda e, qc=qc, p=p: e.activation(
                        alpha[:, p:p + 1], mrun[:, qc:qc + 1], AF.Exp,
                        bias=negm[:, p:p + 1]), inc="sEV")
                    P.wait("vector", "sEV", al_done)
                    dv_here = P.op("vector", lambda e, qc=qc, p=p: e.tensor_copy(
                        mrun[:, qc:qc + 1], mnew[:, p:p + 1]), inc="sDV")
                    P.wait("vector", "sDV", dv_here)
                P.wait("scalar", "sDV", P.cnt["sDV"])
                P.wait("scalar", "sPE", pb_free.get(p))
                for t in range(2):
                    bank = 2 * p + t
                    exp_ev[qc] = P.op("scalar", lambda e, p=p, t=t, b=bank:
                                      e.activation(
                                          Pb[:, p, t * 512:(t + 1) * 512],
                                          att_ps[:, b, :], AF.Exp,
                                          bias=negm[:, p:p + 1],
                                          accum_out=sblk[:, p, t:t + 1]),
                                      inc="sEV")
                    dv_bank[bank] = ("sEV", P.cnt["sEV"])
                P.wait("vector", "sEV", exp_ev[qc])
                if r == 0:
                    P.op("vector", lambda e, qc=qc, p=p: e.tensor_add(
                        srun[:, qc:qc + 1], sblk[:, p, 0:1], sblk[:, p, 1:2]),
                        inc="sDV")
                else:
                    dv = P.op("vector", lambda e, qc=qc, p=p: e.scalar_tensor_tensor(
                        srun[:, qc:qc + 1], srun[:, qc:qc + 1], alpha[:, p:p + 1],
                        sblk[:, p, 0:1], ALU.mult, ALU.add), inc="sDV")
                    P.wait("vector", "sDV", dv)
                    P.op("vector", lambda e, qc=qc, p=p: e.tensor_add(
                        srun[:, qc:qc + 1], srun[:, qc:qc + 1], sblk[:, p, 1:2]),
                        inc="sDV")

            def emit_tail(r, qc):
                vb = kvbuf(r)[1]
                p = qc % 2
                if r == 0 and qc == 0:
                    P.wait("tensor", f"sKV{kv_loaded[0][0]}", kv_loaded[0][1])
                P.wait("tensor", "sEV", exp_ev[qc])
                pt_copy = None
                for kc in range(8):
                    pb = kc % 2
                    P.wait("tensor", "sDV", pt_bank.get(pb))
                    pe_here = P.op(
                        "tensor", lambda e, kc=kc, pb=pb, p=p: e.transpose(
                            pt_ps[:, pb, 0:128], Pb[:, p, kc * 128:(kc + 1) * 128],
                            identb[:, :]),
                        inc="sPE", drain=True)
                    P.wait("vector", "sPE", pe_here)
                    pt_copy = P.op(
                        "vector", lambda e, kc=kc, pb=pb, p=p: e.tensor_copy(
                            PT[:, p, kc * 128:(kc + 1) * 128], pt_ps[:, pb, 0:128]),
                        inc="sDV")
                    pt_bank[pb] = pt_copy
                pb_free[p] = P.cnt["sPE"]
                P.wait("tensor", "sDV", pt_copy)
                for t in range(2):
                    bank = 4 + t
                    fb = dv_bank.get(bank)
                    if fb is not None:
                        P.wait("tensor", fb[0], fb[1])
                    pe_here = None
                    for kc in range(8):
                        pe_here = P.op(
                            "tensor",
                            lambda e, vb=vb, kc=kc, t=t, b=bank, p=p: e.matmul(
                                att_ps[:, b, :],
                                PT[:, p, kc * 128:(kc + 1) * 128],
                                vb[:, kc * 1024 + t * 512:
                                   kc * 1024 + t * 512 + 512],
                                start=(kc == 0), stop=(kc == 7)),
                            inc="sPE" if kc == 7 else None, drain=(kc == 7))
                    P.wait("vector", "sPE", pe_here)
                    if r == 0:
                        dv_bank[bank] = ("sDV", P.op(
                            "vector", lambda e, qc=qc, t=t, b=bank: e.tensor_copy(
                                acc[:, qc * 1024 + t * 512:
                                    qc * 1024 + t * 512 + 512],
                                att_ps[:, b, :]), inc="sDV"))
                    else:
                        dv = P.op(
                            "vector", lambda e, qc=qc, t=t, p=p: e.tensor_scalar_mul(
                                acc[:, qc * 1024 + t * 512:
                                    qc * 1024 + t * 512 + 512],
                                acc[:, qc * 1024 + t * 512:
                                    qc * 1024 + t * 512 + 512],
                                alpha[:, p:p + 1]), inc="sDV")
                        P.wait("vector", "sDV", dv)
                        dv_bank[bank] = ("sDV", P.op(
                            "vector", lambda e, qc=qc, t=t, b=bank: e.tensor_add(
                                acc[:, qc * 1024 + t * 512:
                                    qc * 1024 + t * 512 + 512],
                                acc[:, qc * 1024 + t * 512:
                                    qc * 1024 + t * 512 + 512],
                                att_ps[:, b, :]), inc="sDV"))

            def kvbuf(r):
                return (Kown, Vblk[0]) if r == 0 else (Kblk[r % 2], Vblk[r % 2])

            for r in range(4):
                if r + 1 < 4:
                    P.wait("gpsimd", "sPE", pe_blk.get(r - 1))
                    load_kv(r + 1)
                if r == 0:
                    emit_scores(0, 0)
                for qc in range(NCH):
                    emit_soft(r, qc)
                    if qc + 1 < NCH:
                        emit_scores(r, qc + 1)
                    elif r + 1 < 4:
                        P.wait("tensor", f"sKV{kv_loaded[r + 1][0]}",
                               kv_loaded[r + 1][1])
                        emit_scores(r + 1, 0)
                    emit_tail(r, qc)
                pe_blk[r] = P.cnt["sPE"]

            for qc in range(NCH):
                P.wait("vector", "sDV", P.cnt["sDV"])
                dv_here = P.op("vector", lambda e, qc=qc: e.reciprocal(
                    rsum[:, :], srun[:, qc:qc + 1]), inc="sDV")
                P.wait("vector", "sDV", dv_here)
                P.op("vector", lambda e, qc=qc: e.tensor_scalar_mul(
                    attn_bf[:, qc * 1024:(qc + 1) * 1024],
                    acc[:, qc * 1024:(qc + 1) * 1024], rsum[:, 0:1]), inc="sDV")
            attn_done = P.cnt["sDV"]
            P.wait("tensor", "sDV", attn_done)
            for qc in range(NCH):
                for m in range(8):
                    pb = m % 2
                    P.wait("tensor", "sDV", pt_bank.get(pb))
                    pe_here = P.op("tensor", lambda e, qc=qc, m=m, pb=pb: e.transpose(
                        pt_ps[:, pb, 0:128],
                        attn_bf[:, qc * 1024 + m * 128: qc * 1024 + m * 128 + 128],
                        identb[:, :]), inc="sPE", drain=True)
                    P.wait("vector", "sPE", pe_here)
                    if qc == NCH - 1 and m == 7:
                        P.delay("vector", dly)
                    pt_bank[pb] = P.op(
                        "vector", lambda e, qc=qc, m=m, pb=pb: e.tensor_copy(
                            attnTf[:, m * 1024 + qc * 128: m * 1024 + qc * 128 + 128],
                            pt_ps[:, pb, 0:128]), inc="sDV")

            if "attn" in tap_ap:
                P.wait("gpsimd", "sDV", attn_done)
                P.op("gpsimd", lambda e: e.dma_start(
                    out=tap_ap["attn"].rearrange("(c p) n -> p c n", p=128),
                    in_=attn_bf.rearrange("p (c n) -> p c n", c=8)), inc="sST", k=16)

            if upto == "B":
                P.wait("gpsimd", "sST", P.cnt["sST"])
            P.barrier()

        sB.close()
        stackAB.close()
        if upto == "B":
            return nc

        y1T = SL2[:, :]
        sC = ExitStack()
        wo_sb = sC.enter_context(nc.sbuf_tensor("wo_sb", [128, 2, 4096], BF16))
        bo_rowb = sC.enter_context(nc.sbuf_tensor("bo_rowb", [1, H], BF16))
        xf32 = sC.enter_context(nc.sbuf_tensor("xf32", [128, 2, 1024], F32))
        y1sq = sC.enter_context(nc.sbuf_tensor("y1sq", [128, 3, 1024], BF16))
        s1row = sC.enter_context(nc.sbuf_tensor("s1row", [1, 1024], F32))
        s2row = sC.enter_context(nc.sbuf_tensor("s2row", [1, 1024], F32))
        t1row = sC.enter_context(nc.sbuf_tensor("t1row", [1, 1024], F32))
        m1e = sC.enter_context(nc.sbuf_tensor("m1e", [128, 1024], F32))
        r1e = sC.enter_context(nc.sbuf_tensor("r1e", [128, 1024], F32))
        tmpc = sC.enter_context(nc.sbuf_tensor("tmpc", [128, 1024], F32))
        tmpc2 = sC.enter_context(nc.sbuf_tensor("tmpc2", [128, 1024], F32))
        op_ps = sC.enter_context(nc.psum_tensor("op_ps", [128, 4, 512], F32))
        st_ps = sC.enter_context(nc.psum_tensor("st_ps", [1, 4, 512], F32))
        if True:

            wo_loaded = {}
            xf_loaded = {}
            y1_dv = {}
            sq_ev = {}
            stat_pe = {}
            opb_ev = {}

            def load_wo(hb):
                s = hb % 2
                P.op("gpsimd", lambda e, hb=hb, s=s: e.dma_start(
                    out=wo_sb[:, s, :].rearrange("p (c f) -> p c f", c=8),
                    in_=d_Wo[:, hb * 512:(hb + 1) * 512]
                    .rearrange("(c p) f -> p c f", p=128)), inc=f"sW{s}", k=16)
                wo_loaded[hb] = (s, P.cnt[f"sW{s}"])

            def load_xf(h):
                s = h % 2
                if h - 2 >= 0:
                    P.wait("gpsimd", "sDV", y1_dv.get(h - 2))
                P.op("gpsimd", lambda e, h=h, s=s: e.dma_start(
                    out=xf32[:, s, :], in_=d_xT[h * 128:(h + 1) * 128, :]),
                    inc=f"sXF{s}", k=16)
                xf_loaded[h] = (s, P.cnt[f"sXF{s}"])

            P.op("gpsimd", lambda e: e.dma_start(out=bo_rowb[:, :], in_=d_bor),
                 inc="sW", k=16)
            load_wo(0)
            load_wo(1)
            load_xf(0)
            load_xf(1)

            def stats_for(h):
                P.wait("tensor", "sDV", y1_dv[h])
                for t in range(2):
                    P.op("tensor", lambda e, h=h, t=t: e.matmul(
                        st_ps[0:1, t, :], onesc[:, 0:1],
                        y1T[:, h * 1024 + t * 512: h * 1024 + t * 512 + 512],
                        start=(h == 0), stop=(h == DCH - 1)),
                        inc="sPE" if h == DCH - 1 else None,
                        drain=(h == DCH - 1))
                P.wait("tensor", "sEV", sq_ev[h])
                pe_here = None
                for t in range(2):
                    pe_here = P.op("tensor", lambda e, h=h, t=t: e.matmul(
                        st_ps[0:1, 2 + t, :], onesc[:, 0:1],
                        y1sq[:, h % 3, t * 512:(t + 1) * 512],
                        start=(h == 0), stop=(h == DCH - 1)), inc="sPE",
                        drain=(h == DCH - 1))
                stat_pe[h] = pe_here

            for h in range(DCH):
                hb = h // 4
                s = hb % 2
                P.wait("tensor", f"sW{wo_loaded[hb][0]}", wo_loaded[hb][1])
                for t in range(2):
                    bank = (2 * h + t) % 4
                    P.wait("tensor", "sDV", opb_ev.get(bank))
                    for m in range(8):
                        P.op("tensor", lambda e, h=h, s=s, m=m, t=t, b=bank: e.matmul(
                            op_ps[:, b, :],
                            wo_sb[:, s, m * 512 + (h % 4) * 128:
                                  m * 512 + (h % 4) * 128 + 128],
                            attnTf[:, m * 1024 + t * 512: m * 1024 + t * 512 + 512],
                            start=(m == 0), stop=False))
                    pe_here = P.op("tensor", lambda e, h=h, b=bank: e.matmul(
                        op_ps[:, b, :], bo_rowb[0:1, h * 128:(h + 1) * 128],
                        onesr512[0:1, :], start=False, stop=True),
                        inc="sPE", drain=True)
                    P.wait("vector", "sPE", pe_here)
                    P.wait("vector", f"sXF{xf_loaded[h][0]}", xf_loaded[h][1])
                    opb_ev[bank] = P.op(
                        "vector", lambda e, h=h, t=t, b=bank, s=h % 2:
                        e.tensor_add(
                            y1T[:, h * 1024 + t * 512: h * 1024 + t * 512 + 512],
                            op_ps[:, b, :],
                            xf32[:, s, t * 512:(t + 1) * 512]), inc="sDV")
                y1_dv[h] = P.cnt["sDV"]
                if h + 2 < DCH:
                    load_xf(h + 2)
                P.wait("scalar", "sDV", y1_dv[h])
                P.wait("scalar", "sPE", stat_pe.get(h - 3))
                sq_ev[h] = P.op("scalar", lambda e, h=h: e.activation(
                    y1sq[:, h % 3, :], y1T[:, h * 1024:(h + 1) * 1024], AF.Square),
                    inc="sEV")
                if h - 2 >= 0:
                    stats_for(h - 2)
                if h % 4 == 3 and hb + 2 <= 7:
                    P.wait("gpsimd", "sPE", P.cnt["sPE"])
                    load_wo(hb + 2)
            stats_for(DCH - 2)
            stats_for(DCH - 1)
            stats_pe_done = P.cnt["sPE"]

            tap_y1_st = None
            if "y1T" in tap_ap:
                P.wait("gpsimd", "sDV", P.cnt["sDV"])
                tap_y1_st = P.op("gpsimd", lambda e: e.dma_start(
                    out=tap_ap["y1T"].rearrange("(c p) n -> p c n", p=128),
                    in_=y1T.rearrange("p (c n) -> p c n", c=32)), inc="sST", k=16)

            P.wait("vector", "sPE", stats_pe_done)
            P.op("vector", lambda e: e.tensor_copy(s1row[:, 0:512], st_ps[0:1, 0, :]))
            P.op("vector", lambda e: e.tensor_copy(s1row[:, 512:1024], st_ps[0:1, 1, :]))
            P.op("vector", lambda e: e.tensor_copy(s2row[:, 0:512], st_ps[0:1, 2, :]))
            dv_here = P.op("vector", lambda e: e.tensor_copy(
                s2row[:, 512:1024], st_ps[0:1, 3, :]), inc="sDV")
            P.wait("vector", "sDV", dv_here)
            P.op("vector", lambda e: e.tensor_scalar_mul(
                s1row[:, :], s1row[:, :], 1.0 / H), inc="sDV")
            P.wait("vector", "sDV", P.cnt["sDV"])
            P.op("vector", lambda e: e.tensor_mul(
                t1row[:, :], s1row[:, :], s1row[:, :]), inc="sDV")
            P.op("vector", lambda e: e.tensor_scalar_mul(
                s2row[:, :], s2row[:, :], 1.0 / H), inc="sDV")
            P.wait("vector", "sDV", P.cnt["sDV"])
            P.op("vector", lambda e: e.tensor_sub(
                s2row[:, :], s2row[:, :], t1row[:, :]), inc="sDV")
            P.wait("vector", "sDV", P.cnt["sDV"])
            dv_here = P.op("vector", lambda e: e.tensor_scalar_add(
                s2row[:, :], s2row[:, :], EPS), inc="sDV")
            P.wait("scalar", "sDV", dv_here)
            ev_here = P.op("scalar", lambda e: e.activation(
                t1row[:, :], s2row[:, :], AF.Sqrt), inc="sEV")
            P.wait("vector", "sEV", ev_here)
            rows_done = P.op("vector", lambda e: e.reciprocal(
                t1row[:, :], t1row[:, :]), inc="sDV")

            P.wait("tensor", "sDV", rows_done)
            for b in range(4):
                P.wait("tensor", "sDV", opb_ev.get(b))
            pe_here = None
            for t in range(2):
                P.op("tensor", lambda e, t=t: e.matmul(
                    op_ps[:, t, :], onesrf[0:1, :], s1row[0:1, t * 512:(t + 1) * 512],
                    start=True, stop=True))
                pe_here = P.op("tensor", lambda e, t=t: e.matmul(
                    op_ps[:, 2 + t, :], onesrf[0:1, :],
                    t1row[0:1, t * 512:(t + 1) * 512],
                    start=True, stop=True), inc="sPE", drain=(t == 1))
            P.wait("vector", "sPE", pe_here)
            for t in range(2):
                P.op("vector", lambda e, t=t: e.tensor_copy(
                    m1e[:, t * 512:(t + 1) * 512], op_ps[:, t, :]))
                P.op("vector", lambda e, t=t: e.tensor_copy(
                    r1e[:, t * 512:(t + 1) * 512], op_ps[:, 2 + t, :]), inc="sDV")
            bcast_done = P.cnt["sDV"]

            P.wait("vector", "sDV", bcast_done)
            P.wait("gpsimd", "sDV", bcast_done)
            if tap_y1_st is not None:
                P.wait("vector", "sST", tap_y1_st)
                P.wait("gpsimd", "sST", tap_y1_st)
            for h in range(DCH):
                if h % 2 == 0:
                    dv_here = P.op("vector", lambda e, h=h: e.tensor_sub(
                        tmpc[:, :], y1T[:, h * 1024:(h + 1) * 1024], m1e[:, :]),
                        inc="sDV")
                    P.wait("vector", "sDV", dv_here)
                    dv_here = P.op("vector", lambda e, h=h: e.tensor_mul(
                        y1T[:, h * 1024:(h + 1) * 1024], tmpc[:, :], r1e[:, :]),
                        inc="sDV")
                    P.wait("vector", "sDV", dv_here)
                else:
                    pl_here = P.op("gpsimd", lambda e, h=h: e.tensor_sub(
                        tmpc2[:, :], y1T[:, h * 1024:(h + 1) * 1024], m1e[:, :]),
                        inc="sPL")
                    P.wait("gpsimd", "sPL", pl_here)
                    pl_here = P.op("gpsimd", lambda e, h=h: e.tensor_mul(
                        y1T[:, h * 1024:(h + 1) * 1024], tmpc2[:, :], r1e[:, :]),
                        inc="sPL")
                    P.wait("gpsimd", "sPL", pl_here)
            ln1_done = P.cnt["sDV"]

            if "ln1T" in tap_ap:
                P.wait("gpsimd", "sDV", ln1_done)
                P.wait("gpsimd", "sPL", P.cnt["sPL"])
                P.op("gpsimd", lambda e: e.dma_start(
                    out=tap_ap["ln1T"].rearrange("(c p) n -> p c n", p=128),
                    in_=y1T.rearrange("p (c n) -> p c n", c=32)), inc="sST", k=16)

            if upto == "C":
                P.wait("gpsimd", "sST", P.cnt["sST"])
            P.barrier()

        sC.close()
        if upto == "C":
            return nc
        ln1T = y1T

        y2 = SL1[:, :]
        sD = ExitStack()
        wstr = sD.enter_context(nc.sbuf_tensor("wstr", [128, 2, 4096], BF16))
        bbb = sD.enter_context(nc.sbuf_tensor("bbb2", [1, H], BF16))
        BTb = sD.enter_context(nc.sbuf_tensor("BTb", [128, 4096], BF16))
        hwT = sD.enter_context(nc.sbuf_tensor("hwT", [128, 1024], BF16))
        re_sb = sD.enter_context(nc.sbuf_tensor("re_sb", [128, 1024], F32))
        S1c = sD.enter_context(nc.sbuf_tensor("S1c", [128, 64], F32))
        S2c = sD.enter_context(nc.sbuf_tensor("S2c", [128, 64], F32))
        sqscr = sD.enter_context(nc.sbuf_tensor("sqscr", [128, 512], F32))
        lnt = sD.enter_context(nc.sbuf_tensor("lnt", [128, 6], F32))
        ffn_ps = sD.enter_context(nc.psum_tensor("ffn_ps", [128, 8, 512], F32))
        if True:

            wstr_loaded = {}
            half_pe = {}

            def load_wstr(idx):
                ot, qq = idx // 4, idx % 4
                s = idx % 2
                P.wait("gpsimd", "sPE", half_pe.get(idx - 2))
                P.op("gpsimd", lambda e, ot=ot, qq=qq, s=s: e.dma_start(
                    out=wstr[:, s, :].rearrange("p (c f) -> p c f", c=8),
                    in_=d_WbT[qq * 1024:(qq + 1) * 1024, ot * 512:(ot + 1) * 512]
                    .rearrange("(c p) f -> p c f", p=128)), inc=f"sW{s}", k=16)
                wstr_loaded[idx] = (f"sW{s}", P.cnt[f"sW{s}"])

            load_wstr(0)
            load_wstr(1)

            sG = ExitStack()
            A2b = sG.enter_context(nc.sbuf_tensor("A2b", [128, 4096], BF16))
            lgt = sG.enter_context(nc.sbuf_tensor("lgt", [128, 64], F32))
            rte = sG.enter_context(nc.sbuf_tensor("rte", [128, 64], F32))
            rteT = sG.enter_context(nc.sbuf_tensor("rteT", [8, 1024], F32))
            gm1 = sG.enter_context(nc.sbuf_tensor("gm1", [128, 1], F32))
            gm2 = sG.enter_context(nc.sbuf_tensor("gm2", [128, 1], F32))
            gmsk = sG.enter_context(nc.sbuf_tensor("gmsk", [128, 8], F32))
            gtmp = sG.enter_context(nc.sbuf_tensor("gtmp", [128, 8], F32))
            gsum = sG.enter_context(nc.sbuf_tensor("gsum", [128, 1], F32))
            if True:

                P.op("gpsimd", lambda e: e.dma_start(out=bbb[:, :], in_=d_bb),
                     inc="sW", k=16)
                bbb_w = P.cnt["sW"]
                P.op("gpsimd", lambda e: e.dma_start(
                    out=A2b.rearrange("p (c f) -> p c f", c=32),
                    in_=d_A2.rearrange("(c p) f -> p c f", p=128)), inc="sW", k=16)
                lora_w = P.op("gpsimd", lambda e: e.dma_start(
                    out=BTb[:, :], in_=d_BT), inc="sW", k=16)

                gate_bank = {}
                for ncg in range(8):
                    pb = ncg % 2
                    P.wait("tensor", "sDV", gate_bank.get(pb))
                    pe_here = None
                    for d in range(DCH):
                        pe_here = P.op("tensor", lambda e, d=d, ncg=ncg, pb=pb:
                                       e.matmul(
                                           ffn_ps[:, pb, 0:8],
                                           ln1T[:, d * 1024 + ncg * 128:
                                                d * 1024 + ncg * 128 + 128],
                                           WgTb[:, d * 8:(d + 1) * 8],
                                           start=(d == 0), stop=(d == DCH - 1)),
                                       inc="sPE" if d == DCH - 1 else None,
                                       drain=(d == DCH - 1))
                    P.wait("vector", "sPE", pe_here)
                    gate_bank[pb] = P.op(
                        "vector", lambda e, ncg=ncg, pb=pb: e.tensor_copy(
                            lgt[:, ncg * 8:(ncg + 1) * 8], ffn_ps[:, pb, 0:8]),
                        inc="sDV")
                if upto == "G1":
                    if "route" in tap_ap:
                        P.wait("gpsimd", "sDV", P.cnt["sDV"])
                        P.op("gpsimd", lambda e: e.dma_start(
                            out=tap_ap["route"].rearrange("(c p) f -> p c f", p=128),
                            in_=lgt.rearrange("p (c f) -> p c f", c=8)),
                            inc="sST", k=16)
                    P.wait("gpsimd", "sST", P.cnt["sST"])
                    P.barrier()
                    sG.close()
                    sD.close()
                    return nc

                P.wait("tensor", wstr_loaded[0][0], wstr_loaded[0][1])
                for ncg in range(4, 8):
                    for d in range(8):
                        P.op("tensor", lambda e, d=d, ncg=ncg: e.matmul(
                            ffn_ps[:, ncg, :],
                            ln1T[:, d * 1024 + ncg * 128:
                                 d * 1024 + ncg * 128 + 128],
                            wstr[:, 0, d * 512:(d + 1) * 512],
                            start=(d == 0), stop=False))
                P.wait("tensor", wstr_loaded[1][0], wstr_loaded[1][1])
                for ncg in range(4, 8):
                    for d in range(8):
                        P.op("tensor", lambda e, d=d, ncg=ncg: e.matmul(
                            ffn_ps[:, ncg, :],
                            ln1T[:, (8 + d) * 1024 + ncg * 128:
                                 (8 + d) * 1024 + ncg * 128 + 128],
                            wstr[:, 1, d * 512:(d + 1) * 512],
                            start=False, stop=False))

                for ncg in range(8):
                    P.wait("vector", "sDV", P.cnt["sDV"])
                    dv = P.op("vector", lambda e, ncg=ncg: e.reduce_max(
                        gm1[:, :], lgt[:, ncg * 8:(ncg + 1) * 8], axis=AX.X,
                        negate=True), inc="sDV")
                    P.wait("vector", "sDV", dv)
                    dv = P.op("vector", lambda e, ncg=ncg: e.tensor_scalar(
                        gmsk[:, :], lgt[:, ncg * 8:(ncg + 1) * 8],
                        gm1[:, 0:1], -1.0, ALU.add, ALU.mult), inc="sDV")
                    P.wait("vector", "sDV", dv)
                    dv = P.op("vector", lambda e: e.tensor_single_scalar(
                        gmsk[:, :], gmsk[:, :], 0.0, ALU.is_le), inc="sDV")
                    P.wait("vector", "sDV", dv)
                    dv = P.op("vector", lambda e, ncg=ncg: e.scalar_tensor_tensor(
                        gtmp[:, :], gmsk[:, :], -1e30,
                        lgt[:, ncg * 8:(ncg + 1) * 8], ALU.mult, ALU.add), inc="sDV")
                    P.wait("vector", "sDV", dv)
                    dv = P.op("vector", lambda e: e.reduce_max(
                        gm2[:, :], gtmp[:, :], axis=AX.X), inc="sDV")
                    P.wait("vector", "sDV", dv)
                    dv = P.op("vector", lambda e, ncg=ncg: e.tensor_scalar(
                        gmsk[:, :], lgt[:, ncg * 8:(ncg + 1) * 8],
                        gm2[:, 0:1], None, ALU.is_ge), inc="sDV")
                    P.wait("scalar", "sDV", dv)
                    ev = P.op("scalar", lambda e, ncg=ncg: e.activation(
                        gtmp[:, :], lgt[:, ncg * 8:(ncg + 1) * 8], AF.Exp,
                        bias=gm1[:, 0:1]), inc="sEV")
                    P.wait("vector", "sEV", ev)
                    dv = P.op("vector", lambda e: e.tensor_mul(
                        gtmp[:, :], gtmp[:, :], gmsk[:, :]), inc="sDV")
                    P.wait("vector", "sDV", dv)
                    dv = P.op("vector", lambda e: e.reduce_sum(
                        gsum[:, :], gtmp[:, :], axis=AX.X), inc="sDV")
                    P.wait("vector", "sDV", dv)
                    dv = P.op("vector", lambda e: e.reciprocal(
                        gsum[:, :], gsum[:, :]), inc="sDV")
                    P.wait("vector", "sDV", dv)
                    P.op("vector", lambda e, ncg=ncg: e.tensor_scalar_mul(
                        rte[:, ncg * 8:(ncg + 1) * 8], gtmp[:, :], gsum[:, 0:1]),
                        inc="sDV")
                route_done = P.cnt["sDV"]

                if "route" in tap_ap:
                    P.wait("gpsimd", "sDV", route_done)
                    P.op("gpsimd", lambda e: e.dma_start(
                        out=tap_ap["route"].rearrange("(c p) f -> p c f", p=128),
                        in_=rte.rearrange("p (c f) -> p c f", c=8)), inc="sST", k=16)

                if upto == "G2":
                    P.wait("gpsimd", "sST", P.cnt["sST"])
                    P.barrier()
                    sG.close()
                    sD.close()
                    return nc

                P.wait("tensor", "sDV", route_done)
                for ncg in range(8):
                    pb = ncg % 2
                    P.wait("tensor", "sDV", gate_bank.get(pb))
                    pe_here = P.op("tensor", lambda e, ncg=ncg, pb=pb: e.transpose(
                        ffn_ps[0:8, pb, 0:128],
                        rte[:, ncg * 8:(ncg + 1) * 8], identf[:, :]),
                        inc="sPE", drain=True)
                    P.wait("vector", "sPE", pe_here)
                    gate_bank[pb] = P.op(
                        "vector", lambda e, ncg=ncg, pb=pb: e.tensor_copy(
                            rteT[0:8, ncg * 128:(ncg + 1) * 128],
                            ffn_ps[0:8, pb, 0:128]), inc="sDV")
                rteT_done = P.cnt["sDV"]

                if upto == "G3":
                    P.wait("gpsimd", "sST", P.cnt["sST"])
                    P.barrier()
                    sG.close()
                    sD.close()
                    return nc

                P.wait("tensor", "sDV", rteT_done)
                re_done = None
                re_evt = {}
                for t in range(2):
                    pe_here = P.op("tensor", lambda e, t=t: e.matmul(
                        ffn_ps[:, 2 + t, :], EXP8b[:, :],
                        rteT[0:8, t * 512:(t + 1) * 512],
                        start=True, stop=True), inc="sPE", drain=True)
                    P.wait("vector", "sPE", pe_here)
                    re_done = P.op("vector", lambda e, t=t: e.tensor_copy(
                        re_sb[:, t * 512:(t + 1) * 512], ffn_ps[:, 2 + t, :]),
                        inc="sDV")
                    re_evt[t] = re_done
                if upto == "G4":
                    P.wait("gpsimd", "sDV", P.cnt["sDV"])
                    P.wait("gpsimd", "sST", P.cnt["sST"])
                    P.barrier()
                    sG.close()
                    sD.close()
                    return nc

                P.wait("tensor", "sW", lora_w)
                for t in range(2):
                    P.wait("tensor", "sDV", re_evt[t])
                    pe_here = None
                    for d in range(DCH):
                        pe_here = P.op("tensor", lambda e, d=d, t=t: e.matmul(
                            ffn_ps[:, 2 + t, :], A2b[:, d * 128:(d + 1) * 128],
                            ln1T[:, d * 1024 + t * 512: d * 1024 + t * 512 + 512],
                            start=(d == 0), stop=(d == DCH - 1)),
                            inc="sPE" if d == DCH - 1 else None,
                            drain=(d == DCH - 1))
                    P.wait("vector", "sPE", pe_here)
                    P.wait("vector", "sDV", re_done)
                    P.op("vector", lambda e, t=t: e.tensor_mul(
                        hwT[:, t * 512:(t + 1) * 512], ffn_ps[:, 2 + t, :],
                        re_sb[:, t * 512:(t + 1) * 512]), inc="sDV")
                hw_done = P.cnt["sDV"]
                if "hwT" in tap_ap:
                    P.wait("gpsimd", "sDV", hw_done)
                    P.op("gpsimd", lambda e: e.dma_start(
                        out=tap_ap["hwT"].rearrange("(c p) n -> p c n", p=128),
                        in_=hwT.rearrange("p (c n) -> p c n", c=1)), inc="sST", k=16)

            sG.close()

            if upto == "G":
                P.wait("gpsimd", "sST", P.cnt["sST"])
                P.barrier()
                sD.close()
                return nc

            evac_dv = {}
            sq_last = None
            for ot in range(8):
                for qq in range(4):
                    idx = ot * 4 + qq
                    slot = idx % 2
                    P.wait("tensor", wstr_loaded[idx][0], wstr_loaded[idx][1])
                    if ot == 0 and qq == 0:
                        P.wait("tensor", "sDV", rteT_done)
                        P.wait("tensor", "sDV", hw_done)
                    if qq == 0 and ot > 0:
                        for ncg in range(8):
                            P.wait("tensor", "sDV", evac_dv.get(ncg))
                    for ncg in range(8):
                        if ot == 0 and qq <= 1 and ncg >= 4:
                            continue
                        for d in range(8):
                            dg = qq * 8 + d
                            last = (d == 7 and
                                    ncg == (3 if (ot == 0 and qq <= 1) else 7))
                            half_pe[idx] = P.op(
                                "tensor",
                                lambda e, s=slot, d=d, dg=dg, ncg=ncg: e.matmul(
                                    ffn_ps[:, ncg, :],
                                    ln1T[:, dg * 1024 + ncg * 128:
                                         dg * 1024 + ncg * 128 + 128],
                                    wstr[:, s, d * 512:(d + 1) * 512],
                                    start=(dg == 0), stop=False),
                                inc="sPE" if last else None)
                    if idx + 2 < 32:
                        load_wstr(idx + 2)
                for ncg in range(8):
                    P.op("tensor", lambda e, ncg=ncg, ot=ot: e.matmul(
                        ffn_ps[:, ncg, :], hwT[:, ncg * 128:(ncg + 1) * 128],
                        BTb[:, ot * 512:(ot + 1) * 512], start=False, stop=False))
                    if ot == 0 and ncg == 0:
                        P.wait("tensor", "sW", bbb_w)
                    P.op("tensor", lambda e, ncg=ncg, ot=ot: e.matmul(
                        ffn_ps[:, ncg, :], onesr[0:1, :],
                        bbb[0:1, ot * 512:(ot + 1) * 512], start=False, stop=False))
                    pe_here = None
                    for j in range(4):
                        pe_here = P.op("tensor", lambda e, ncg=ncg, ot=ot, j=j: e.matmul(
                            ffn_ps[:, ncg, j * 128:(j + 1) * 128],
                            ln1T[:, (ot * 4 + j) * 1024 + ncg * 128:
                                 (ot * 4 + j) * 1024 + ncg * 128 + 128],
                            identb[:, :], start=False, stop=(j == 3)),
                            inc="sPE" if j == 3 else None, drain=(j == 3))
                    P.wait("scalar", "sPE", pe_here)
                    if ot == 7 and ncg == 7:
                        P.delay("scalar", dly)
                    sq_last = P.op("scalar", lambda e, ncg=ncg, ot=ot: e.activation(
                        sqscr[:, :], ffn_ps[:, ncg, :], AF.Square,
                        accum_out=S2c[:, ncg * 8 + ot: ncg * 8 + ot + 1]), inc="sEV")
                    P.wait("vector", "sEV", sq_last)
                    evac_dv[ncg] = P.op(
                        "vector", lambda e, ncg=ncg, ot=ot: e.tensor_scalar(
                            y2[:, ncg * 4096 + ot * 512: ncg * 4096 + ot * 512 + 512],
                            ffn_ps[:, ncg, :], 1.0, 0.0, ALU.mult, ALU.add,
                            accum_out=S1c[:, ncg * 8 + ot: ncg * 8 + ot + 1]),
                        inc="sDV")
            ffn_done = P.cnt["sDV"]

            if "y2" in tap_ap:
                P.wait("gpsimd", "sDV", ffn_done)
                P.op("gpsimd", lambda e: e.dma_start(
                    out=tap_ap["y2"].rearrange("(c p) f -> p c f", p=128),
                    in_=y2.rearrange("p (c f) -> p c f", c=8)), inc="sST", k=16)

            outst = sD.enter_context(nc.sbuf_tensor("outst", [128, 2, 1024], F32))
            if True:
                st_done = {}
                P.wait("vector", "sDV", ffn_done)
                P.wait("vector", "sEV", sq_last)
                for ncg in range(8):
                    P.op("vector", lambda e, ncg=ncg: e.reduce_sum(
                        lnt[:, 0:1], S1c[:, ncg * 8:(ncg + 1) * 8], axis=AX.X),
                        inc="sDV")
                    dv = P.op("vector", lambda e, ncg=ncg: e.reduce_sum(
                        lnt[:, 1:2], S2c[:, ncg * 8:(ncg + 1) * 8], axis=AX.X),
                        inc="sDV")
                    P.wait("vector", "sDV", dv)
                    P.op("vector", lambda e: e.tensor_scalar_mul(
                        lnt[:, 0:1], lnt[:, 0:1], 1.0 / H), inc="sDV")
                    dv = P.op("vector", lambda e: e.tensor_scalar_mul(
                        lnt[:, 1:2], lnt[:, 1:2], 1.0 / H), inc="sDV")
                    P.wait("vector", "sDV", dv)
                    dv = P.op("vector", lambda e: e.tensor_mul(
                        lnt[:, 2:3], lnt[:, 0:1], lnt[:, 0:1]), inc="sDV")
                    P.wait("vector", "sDV", dv)
                    dv = P.op("vector", lambda e: e.tensor_sub(
                        lnt[:, 3:4], lnt[:, 1:2], lnt[:, 2:3]), inc="sDV")
                    P.wait("vector", "sDV", dv)
                    dv = P.op("vector", lambda e: e.tensor_scalar_add(
                        lnt[:, 3:4], lnt[:, 3:4], EPS), inc="sDV")
                    P.wait("scalar", "sDV", dv)
                    ev = P.op("scalar", lambda e: e.activation(
                        lnt[:, 4:5], lnt[:, 3:4], AF.Sqrt), inc="sEV")
                    P.wait("vector", "sEV", ev)
                    dv = P.op("vector", lambda e: e.reciprocal(
                        lnt[:, 5:6], lnt[:, 4:5]), inc="sDV")
                    P.wait("vector", "sDV", dv)
                    for q4 in range(4):
                        s = q4 % 2
                        P.wait("vector", f"sSO{s}", st_done.get(s))
                        dv = P.op("vector", lambda e, ncg=ncg, q4=q4, s=s:
                                  e.tensor_scalar(
                                      outst[:, s, :],
                                      y2[:, ncg * 4096 + q4 * 1024:
                                         ncg * 4096 + q4 * 1024 + 1024],
                                      lnt[:, 0:1], lnt[:, 5:6],
                                      ALU.subtract, ALU.mult), inc="sDV")
                        P.wait("sync", "sDV", dv)
                        st_done[s] = P.op(
                            "sync", lambda e, ncg=ncg, q4=q4, s=s: e.dma_start(
                                out=d_out[ncg * 128:(ncg + 1) * 128,
                                          q4 * 1024:(q4 + 1) * 1024],
                                in_=outst[:, s, :]), inc=f"sSO{s}", k=16)

                P.wait("gpsimd", "sSO0", P.cnt["sSO0"])
                P.wait("gpsimd", "sSO1", P.cnt["sSO1"])
                P.wait("gpsimd", "sST", P.cnt["sST"])
                P.barrier()
        sD.close()

    return nc


def _prep_inputs(L0_attn_x, L0_attn_Wqkv, L0_attn_bqkv, L0_attn_Wo, L0_attn_bo,
                 L0_ffn_W_base, L0_ffn_b_base, L0_ffn_W_gate, L0_ffn_A, L0_ffn_B):
    bf = ml_dtypes.bfloat16
    f32 = np.float32
    com = {
        "Wqkv": np.ascontiguousarray(np.asarray(L0_attn_Wqkv, dtype=f32)),
        "bqkv_pc": np.ascontiguousarray(np.asarray(L0_attn_bqkv, dtype=f32)
                                        .reshape(24, 128).T),
        "bv_row": np.ascontiguousarray(np.asarray(L0_attn_bqkv, dtype=f32)
                                       [2048:3072].reshape(1, DH)),
        "Wo": np.ascontiguousarray(np.asarray(L0_attn_Wo, dtype=f32)),
        "bo_pc": np.ascontiguousarray(np.asarray(L0_attn_bo, dtype=f32)
                                      .reshape(32, 128).T),
        "WbT": np.ascontiguousarray(np.asarray(L0_ffn_W_base, dtype=f32).T),
        "bb_row": np.ascontiguousarray(np.asarray(L0_ffn_b_base, dtype=f32)
                                       .reshape(1, H)),
        "WgT": np.ascontiguousarray(np.asarray(L0_ffn_W_gate, dtype=f32).T),
        "A2": np.ascontiguousarray(np.asarray(L0_ffn_A, dtype=f32)
                                   .transpose(2, 0, 1).reshape(H, 128)),
        "BT": np.ascontiguousarray(np.asarray(L0_ffn_B, dtype=f32)
                                   .transpose(0, 2, 1).reshape(128, H)),
        "EXP8": np.kron(np.eye(8, dtype=f32), np.ones((1, 16), dtype=f32)),
        "identb": np.eye(128, dtype=bf),
        "identf": np.eye(128, dtype=f32),
        "onescol": np.ones((128, 1), dtype=bf),
        "onesrow": np.ones((1, 128), dtype=bf),
        "onesrow512": np.ones((1, 512), dtype=bf),
        "bo_row": np.ascontiguousarray(np.asarray(L0_attn_bo, dtype=f32)
                                       .reshape(1, H)),
        "onesrowf": np.ones((1, 128), dtype=f32),
    }
    x = np.asarray(L0_attn_x, dtype=f32)
    in_maps = []
    for c in range(8):
        b, q0 = c // 4, (c % 4) * NT
        m = dict(com)
        m["xT"] = np.ascontiguousarray(x[b].T[:, q0:q0 + NT])
        g, my = c // 4, c % 4
        off = [0] * 4
        for j in (0, 1, 2, 3):
            rank = 4 * g + (my + j) % 4
            off[j] = rank * 2 * DH * NT
        m["kvoff"] = np.asarray([off], dtype=np.int32)
        in_maps.append(m)
    return in_maps


def kernel(**inputs):
    in_maps = _prep_inputs(**inputs)
    nc = build_nc(taps=DEBUG_TAPS, upto=UPTO)
    res = run_bass_kernel_spmd(nc, in_maps, core_ids=list(range(8)))
    out = np.zeros((2, S, H), dtype=np.float32)
    if UPTO == "D":
        for c in range(8):
            b, q0 = c // 4, (c % 4) * NT
            out[b, q0:q0 + NT, :] = np.asarray(res.results[c]["out"])
    if DEBUG_TAPS:
        kernel.last_results = res
    return out
